# version 46
# baseline (speedup 1.0000x reference)
import math
import os
import numpy as np
import ml_dtypes
import concourse.bass as bass
import concourse.mybir as mybir
from concourse.bass_utils import run_bass_kernel_spmd

F32 = mybir.dt.float32
BF16 = mybir.dt.bfloat16
I32 = mybir.dt.int32
AF = mybir.ActivationFunctionType
ALU = mybir.AluOpType
AX = mybir.AxisListType

N = 2048
D = 1024
NTT = 16
NDC = 8
DFF = 512
NE = 16
FDFF = 1024

RFF_SCALE = math.sqrt(2.0 / 64.0) * math.sqrt(math.exp(-0.5))
MAGS = [math.sqrt(abs((-1.0) ** m / math.factorial(m)) + 1e-10) for m in range(5)]
TWO_PI = 2.0 * math.pi
C1 = 6.28125
C2 = TWO_PI - C1
PI_LO = float(np.nextafter(np.float32(np.pi), np.float32(0.0)))
DECAY = 0.03

DEBUG = bool(int(os.environ.get("KDEBUG", "0")))
KPHASE = os.environ.get("KPHASE", "")
KOPT = os.environ.get("KOPT", "")

ENGN = ("pe", "act", "dve", "pool", "sp")
QUEUES = ("sp", "act", "pool")
NLANES = 8


class Op:
    __slots__ = ("eng", "emit", "waits", "inc", "vc", "idx", "tag", "comp")

    def __init__(self, eng, emit, tag):
        self.eng = eng
        self.emit = emit
        self.waits = []
        self.inc = None
        self.vc = {}
        self.idx = -1
        self.tag = tag
        self.comp = None


class Sched:
    def __init__(self, nc):
        self.nc = nc
        self.seq = []
        self.regions = {}
        self.esem = {e: nc.alloc_semaphore(f"es_{e}") for e in ("pe", "act", "dve", "pool")}
        self.ecount = {e: 0 for e in ("pe", "act", "dve", "pool")}
        self.lanes = {q: [nc.alloc_semaphore(f"dl_{q}{i}") for i in range(NLANES)] for q in QUEUES}
        self.lane_uses = {q: [0] * NLANES for q in QUEUES}
        self.lane_last = {q: [None] * NLANES for q in QUEUES}
        self.dma_idx = {q: 0 for q in QUEUES}
        self.evc = {e: {} for e in ENGN}

    def _reg(self, ap):
        if not hasattr(ap, "tensor"):
            ap = ap[tuple(slice(None) for _ in ap.shape)]
        t = ap.tensor
        dims = [list(d) for d in ap.ap]
        off = ap.offset
        assert isinstance(off, int), f"dynamic offset unsupported: {ap}"
        es = mybir.dt.size(ap.dtype)
        sp = str(ap.space).upper()
        if "DRAM" in sp or "HBM" in sp:
            ext = sum(abs(s) * (c - 1) for s, c in dims)
            return (t.name, off * es, (off + ext + 1) * es)
        if "PSUM" in sp:
            return (t.name, 0, 1 << 30)
        row = dims[0][0] if dims else 1
        ext = sum(abs(s) * (c - 1) for s, c in dims[1:]) if len(dims) > 1 else 0
        lo = off % row if row > 0 else off
        return (t.name, lo * es, (lo + ext + 1) * es)

    def _deps_update(self, op, reads, writes):
        deps = set()
        for (nm, lo, hi) in reads:
            recs = self.regions.get(nm)
            if recs:
                for r in recs:
                    if r[0] == "w" and r[1] < hi and lo < r[2]:
                        deps.add(r[3])
        for (nm, lo, hi) in writes:
            recs = self.regions.get(nm)
            if recs:
                for r in recs:
                    if r[1] < hi and lo < r[2]:
                        deps.add(r[3])
        for (nm, lo, hi) in reads:
            self.regions.setdefault(nm, []).append(("r", lo, hi, op))
        for (nm, lo, hi) in writes:
            recs = self.regions.setdefault(nm, [])
            recs[:] = [r for r in recs if not (lo <= r[1] and r[2] <= hi)]
            recs.append(("w", lo, hi, op))
        return deps

    def _resolve(self, op, deps):
        evc = self.evc[op.eng]
        pe_sem = self.esem["pe"]
        for d in sorted(deps, key=lambda o: o.idx):
            s, v = d.comp
            if evc.get(s, 0) >= v:
                continue
            if op.eng == "pe" and d.eng == "pe" and s is pe_sem:
                evc[s] = v
                for ks, kv in d.vc.items():
                    if evc.get(ks, 0) < kv:
                        evc[ks] = kv
                continue
            op.waits.append((s, v))
            evc[s] = v
            for ks, kv in d.vc.items():
                if evc.get(ks, 0) < kv:
                    evc[ks] = kv
        op.vc = dict(evc)
        if op.comp is not None:
            op.vc[op.comp[0]] = op.comp[1]

    def add(self, eng, emit, rd, wr, tag=""):
        op = Op(eng, emit, tag)
        op.idx = len(self.seq)
        reads = [self._reg(a) for a in rd]
        writes = [self._reg(a) for a in wr]
        deps = self._deps_update(op, reads, writes)
        self.ecount[eng] += 1
        op.comp = (self.esem[eng], self.ecount[eng])
        op.inc = (self.esem[eng], 1)
        self._resolve(op, deps)
        self.seq.append(op)
        return op

    def add_dma(self, queue, emit, rd, wr, tag=""):
        eng = queue
        op = Op(eng, emit, tag)
        op.idx = len(self.seq)
        reads = [self._reg(a) for a in rd]
        writes = [self._reg(a) for a in wr]
        deps = self._deps_update(op, reads, writes)
        lane = self.dma_idx[queue] % NLANES
        self.dma_idx[queue] += 1
        prev = self.lane_last[queue][lane]
        if prev is not None:
            deps.add(prev)
        self.lane_uses[queue][lane] += 1
        sem = self.lanes[queue][lane]
        op.comp = (sem, 16 * self.lane_uses[queue][lane])
        op.inc = (sem, 16)
        self.lane_last[queue][lane] = op
        self._resolve(op, deps)
        self.seq.append(op)
        return op

    def final_barrier(self):
        waits = []
        for q in QUEUES:
            for i in range(NLANES):
                if self.lane_uses[q][i]:
                    waits.append((self.lanes[q][i], 16 * self.lane_uses[q][i]))
        for e in ("pe", "act", "dve", "pool"):
            if self.ecount[e]:
                waits.append((self.esem[e], self.ecount[e]))
        op = Op("sp", None, "final")
        op.idx = len(self.seq)
        op.waits = waits
        self.seq.append(op)

    def emit_all(self):
        nc = self.nc
        streams = {e: [o for o in self.seq if o.eng == e] for e in ENGN}

        def run(stream, engobj):
            for o in stream:
                for (s, v) in o.waits:
                    engobj.wait_ge(s, v)
                if o.emit is None:
                    continue
                ins = o.emit(engobj)
                if o.inc is not None:
                    assert ins is not None, f"op {o.tag} returned no instruction"
                    ins.then_inc(o.inc[0], o.inc[1])

        with nc.Block() as block:
            if streams["pe"]:
                block.tensor(lambda e: run(streams["pe"], e))
            if streams["act"]:
                block.scalar(lambda e: run(streams["act"], e))
            if streams["dve"]:
                block.vector(lambda e: run(streams["dve"], e))
            if streams["pool"]:
                block.gpsimd(lambda e: run(streams["pool"], e))
            if streams["sp"]:
                block.sync(lambda e: run(streams["sp"], e))


WEIGHT_SHAPES = {
    "L0_attn_W_charge": [1, D], "L0_attn_b_charge": [1], "L0_attn_W_shell": [64, D],
    "L0_attn_b_shell": [64], "L0_attn_omega": [32, 64], "L0_attn_rff_bias": [32],
    "L0_attn_W_combine": [64, 320], "L0_attn_W_v": [D, D], "L0_attn_b_v": [D],
    "L0_attn_W_out": [D, D], "L0_attn_b_out": [D], "L0_attn_log_temp": [1],
    "L0_ffn_gate_w": [NE, D], "L0_ffn_expert_bias": [NE],
    "L0_ffn_w1": [NE, DFF, D], "L0_ffn_b1": [NE, DFF],
    "L0_ffn_w2": [NE, D, DFF], "L0_ffn_b2": [NE, D],
    "L0_ffn_w3": [NE, DFF, D], "L0_ffn_b3": [NE, DFF],
    "L0_ffn_sw1": [FDFF, D], "L0_ffn_sb1": [FDFF],
    "L0_ffn_sw2": [D, FDFF], "L0_ffn_sb2": [D],
    "L0_ffn_sw3": [FDFF, D], "L0_ffn_sb3": [FDFF],
}


def build_nc():
    nc = bass.Bass("TRN2", target_bir_lowering=False)
    S = Sched(nc)

    x_in = nc.dram_tensor("x_in", [N, D], F32, kind="ExternalInput").ap()
    out = nc.dram_tensor("out", [N, D], F32, kind="ExternalOutput").ap()
    W = {k: nc.dram_tensor(k, shp, F32, kind="ExternalInput").ap() for k, shp in WEIGHT_SHAPES.items()}

    wv_bf = nc.dram_tensor("wv_bf", [D, D], BF16, kind="Internal").ap()
    wo_bf = nc.dram_tensor("wo_bf", [D, D], BF16, kind="Internal").ap()
    w1_bf = nc.dram_tensor("w1_bf", [NE, DFF, D], BF16, kind="Internal").ap()
    w3_bf = nc.dram_tensor("w3_bf", [NE, DFF, D], BF16, kind="Internal").ap()
    w2_bf = nc.dram_tensor("w2_bf", [NE, D, DFF], BF16, kind="Internal").ap()
    sw1_bf = nc.dram_tensor("sw1_bf", [FDFF, D], BF16, kind="Internal").ap()
    sw3_bf = nc.dram_tensor("sw3_bf", [FDFF, D], BF16, kind="Internal").ap()
    sw2_bf = nc.dram_tensor("sw2_bf", [D, FDFF], BF16, kind="Internal").ap()
    rrow_dr = nc.dram_tensor("rrow_dr", [1, N], F32, kind="Internal").ap()

    cmv = np.zeros((128, 128), np.float32)
    cmv[np.triu_indices(128, 1)] = -1e30
    cmask_d = nc.inline_tensor(cmv.astype(ml_dtypes.bfloat16), "c_mask").ap()
    CONSTVALS = [1e-5, 1.0, math.pi / 2, -0.5, -1.0, 0.5, 0.0, 0.1,
                 12582912.0, -12582912.0, 0.0, 0.0]
    cv = np.tile(np.array(CONSTVALS, np.float32)[None, :], (128, 1))
    cvals_d = nc.inline_tensor(cv, "c_cvals").ap()
    pos = np.arange(N, dtype=np.float64)
    qexp_d = nc.inline_tensor(
        np.exp(-DECAY * pos)[None, :].astype(ml_dtypes.bfloat16).copy(), "c_qexp").ap()
    kexp_d = nc.inline_tensor(
        np.exp(DECAY * pos)[None, :].astype(ml_dtypes.bfloat16).copy(), "c_kexp").ap()

    A = nc.alloc_sbuf_tensor
    xres = A("xres", [128, NTT * D], F32)
    fmA = A("fmA", [128, NDC * N], BF16)
    fmB = A("fmB", [128, NTT * D], BF16)
    warena = A("warena", [128, 20480], BF16)
    hbuf = A("hbuf", [128, 8192], BF16)
    scr8 = A("scr8", [128, 2048], F32)
    pPT = A("pPT", [128, 2048], BF16)
    stage = A("stage", [128, 1024], BF16)
    s1a = A("s1a", [128, 512], F32)
    s1c = A("s1c", [128, 512], F32)
    ebias_bc = A("ebias_bc", [128, 16], F32)
    bshb = A("bshb", [128, 64], F32)
    bb1 = A("bb1", [128, 64], F32)
    bb3 = A("bb3", [128, 64], F32)
    bsb1 = A("bsb1", [128, 8], F32)
    bsb3 = A("bsb3", [128, 8], F32)
    idbf = A("idbf", [128, 128], BF16)
    idf = A("idf", [128, 128], F32)
    cmask = A("cmask", [128, 128], BF16)
    ones1 = A("ones1", [128, 128], BF16)
    cols = A("cols", [128, 16], F32)
    bno = A("bno", [128, 24], F32)
    aggo = A("aggo", [128, 4], F32)
    lncol = A("lncol", [128, 4], F32)
    shst = A("shst", [128, 64], BF16)
    shst2 = A("shst2", [128, 64], BF16)
    stage2 = A("stage2", [128, 1024], BF16)
    wsm = A("wsm", [128, 384], BF16)
    cvals = A("cvals", [128, 12], F32)
    sqs = s1a[:, 64:128]
    b2bf = hbuf[0:16, 0:1024]
    gatew = scr8[:, 512:768]
    gstat = scr8[:, 768:832]
    gatewT = scr8[:, 1024:2048].bitcast(BF16)
    shtm = scr8[:, 1536:1600]
    shtm2 = scr8[:, 1600:1664]

    PT_ = nc.alloc_psum_tensor
    psA = PT_("psA", [128, 512], F32)
    psB = PT_("psB", [128, 512], F32)
    psG = PT_("psG", [128, 512], F32)
    psH = PT_("psH", [128, 512], F32)
    psC = PT_("psC", [128, 512], F32)
    psD = PT_("psD", [128, 512], F32)
    psE = PT_("psE", [128, 1024], BF16)
    psF = PT_("psF", [128, 512], F32)

    dbg = {}
    if DEBUG:
        shapes_d = {"d_scrq": [128, 2048], "d_phif": [128, 2048], "d_srow": [128, 2048],
                    "d_gatew": [128, 256], "d_xattn": [128, 1024], "d_wsm": [128, 384],
                    "d_lnt": [128, 2048], "d_logits": [128, 256], "d_nats": [128, 1024]}
        for nm, shp in shapes_d.items():
            dbg[nm] = nc.dram_tensor(nm, shp, F32, kind="ExternalOutput").ap()

    def mm_group(ps_ap, pairs, tag):
        pairs = list(pairs)

        def emit(t, ps_ap=ps_ap, pairs=pairs):
            n = len(pairs)
            last = None
            for i, (l, r) in enumerate(pairs):
                last = t.matmul(ps_ap, l, r, start=(i == 0), stop=(i == n - 1))
            return last
        rd = [l for l, _ in pairs] + [r for _, r in pairs]
        return S.add("pe", emit, rd=rd, wr=[ps_ap], tag=tag)

    def mm_group2(psc, psd, lhsts, rhs_c, rhs_d, tag, pre_c=(), pre_d=(),
                  open_end=False, cont=False):
        psc = psc[:, :] if not hasattr(psc, "tensor") else psc
        psd = psd[:, :] if not hasattr(psd, "tensor") else psd
        lhsts = list(lhsts)
        rhs_c = list(rhs_c)
        rhs_d = list(rhs_d)
        pre_c = list(pre_c)
        pre_d = list(pre_d)

        def emit(t, psc=psc, psd=psd, lhsts=lhsts, rhs_c=rhs_c, rhs_d=rhs_d,
                 pre_c=pre_c, pre_d=pre_d, open_end=open_end, cont=cont):
            last = None
            for i, (l, r) in enumerate(pre_c):
                t.matmul(psc, l, r, start=(i == 0) and not cont, stop=False)
            for i, (l, r) in enumerate(pre_d):
                t.matmul(psd, l, r, start=(i == 0) and not cont, stop=False)
            n = len(lhsts)
            last = None
            for i, l in enumerate(lhsts):
                sc = (i == 0) and not pre_c and not cont
                sd = (i == 0) and not pre_d and not cont
                stp = (i == n - 1) and not open_end
                t.matmul(psc, l, rhs_c[i], start=sc, stop=stp)
                last = t.matmul(psd, l, rhs_d[i], start=sd, stop=stp)
            return last
        rd = lhsts + rhs_c + rhs_d + [a for p in pre_c for a in p] + [a for p in pre_d for a in p]
        return S.add("pe", emit, rd=rd, wr=[psc, psd], tag=tag)

    def tr_group(ps_slices, in_slices, iden, tag):
        ps_slices = list(ps_slices)
        in_slices = list(in_slices)
        idens = []
        for i in in_slices:
            b = i.base_partition()
            p = i.shape[0]
            idens.append(iden[b:b + p, b:b + p])

        def emit(t, ps_slices=ps_slices, in_slices=in_slices, idens=idens):
            last = None
            for o, i, idn in zip(ps_slices, in_slices, idens):
                last = t.transpose(o, i, idn)
            return last
        return S.add("pe", emit, rd=in_slices + [iden], wr=ps_slices, tag=tag)

    def dve(emit, rd, wr, tag=""):
        return S.add("dve", emit, rd=rd, wr=wr, tag=tag)

    def pool(emit, rd, wr, tag=""):
        return S.add("pool", emit, rd=rd, wr=wr, tag=tag)

    def act(emit, rd, wr, tag=""):
        return S.add("act", emit, rd=list(rd) + [cvals[:, :]], wr=wr, tag=tag)

    def dma(q, out_ap, in_ap, tag="", transpose=False):
        if transpose:
            return S.add_dma(q, lambda e, o=out_ap, i=in_ap: e.dma_start_transpose(o, i),
                             rd=[in_ap], wr=[out_ap], tag=tag)
        return S.add_dma(q, lambda e, o=out_ap, i=in_ap: e.dma_start(out=o, in_=i),
                         rd=[in_ap], wr=[out_ap], tag=tag)

    def ddump(name, src_ap, q="act"):
        if DEBUG and name in dbg:
            S.add_dma(q, lambda e, a=src_ap, o=dbg[name]: e.dma_start(out=o, in_=a),
                      rd=[src_ap], wr=[dbg[name]], tag=f"dbg_{name}")

    def xload(g, q="pool"):
        src = x_in[g * 512:(g + 1) * 512, :].rearrange("(t p) d -> p t d", p=128)
        dst = xres[:, g * 4096:(g + 1) * 4096].rearrange("p (t d) -> p t d", d=D)
        dma(q, dst, src, tag=f"xload{g}")
    dve(lambda e: e.memset(ones1[:, :], 1.0), rd=[], wr=[ones1[:, :]], tag="ones")
    dma("pool", pPT[0:1, 0:1024], W["L0_attn_W_charge"], tag="wch_nat")
    dma("pool", pPT[32:48, 0:1024], W["L0_ffn_gate_w"], tag="gw_nat")
    dma("pool", pPT[64:96, 0:64], W["L0_attn_omega"], tag="om_nat")
    dma("pool", pPT[0:64, 1024:1344], W["L0_attn_W_combine"], tag="wc_nat")
    pool(lambda e: e.affine_select(idbf[:, :], ones1[:, :], [[1, 128]], ALU.is_equal,
                                   0.0, channel_multiplier=-1),
         rd=[ones1[:, :]], wr=[idbf[:, :]], tag="idbf")
    dve(lambda e: e.tensor_copy(idf[:, :], idbf[:, :]), rd=[idbf[:, :]], wr=[idf[:, :]],
        tag="idf")
    xload(0)
    dma("pool", hbuf[0:64, 4096:5120], W["L0_attn_W_shell"], tag="wsh_stage")
    dma("pool", wv_bf, W["L0_attn_W_v"], tag="cast_wv")

    dma("sp", cvals[:, :], cvals_d, tag="cvals")
    for _i, _v in enumerate(CONSTVALS):
        nc.const_aps.register(F32, _v, cvals[:, _i:_i + 1])
    if KPHASE == "p_x":
        S.final_barrier()
        S.emit_all()
        return nc


    ddump("d_nats", pPT[:, 0:1024], q="pool")

    dma("act", scr8[32:40, 0:128], W["L0_ffn_sb1"].rearrange("(a b) -> a b", b=128), tag="sb1nat")
    dma("act", scr8[64:72, 0:128], W["L0_ffn_sb3"].rearrange("(a b) -> a b", b=128), tag="sb3nat")
    dma("act", scr8[0:16, 0:512], W["L0_ffn_b1"], tag="b1nat")
    dma("act", scr8[0:16, 512:1024], W["L0_ffn_b3"], tag="b3nat")
    dma("act", bshb[:, :], W["L0_attn_b_shell"][None, :].partition_broadcast(128), tag="bshb")
    dma("act", cols[0:1, 0:1], W["L0_attn_b_charge"][None, :], tag="bch")
    dma("act", cols[0:1, 4:5], W["L0_attn_log_temp"][None, :], tag="lt")
    dma("act", ebias_bc[:, :], W["L0_ffn_expert_bias"][None, :].partition_broadcast(128), tag="ebias")
    dma("act", cols[0:32, 2:3], W["L0_attn_rff_bias"].rearrange("(p o) -> p o", o=1), tag="rffcol")
    dma("act", cols[32:64, 2:3], W["L0_attn_rff_bias"].rearrange("(p o) -> p o", o=1), tag="rffcol2")
    for g in range(1, 4):
        xload(g, "act")
    dma("act", hbuf[0:64, 6144:8192], qexp_d[0:1, :].partition_broadcast(64), tag="qexp")
    if KPHASE == "p_rows":
        S.final_barrier()
        S.emit_all()
        return nc


    WVT, WOT, WSHT = 0, 8192, 16384

    tr_group([psE[:, dc * 128: dc * 128 + 1] for dc in range(8)],
             [pPT[0:1, dc * 128:(dc + 1) * 128] for dc in range(8)], idbf[:, :], "tr_wch")
    dve(lambda e: e.tensor_copy(wsm[:, 0:8].rearrange("p (a b) -> p a b", b=1),
                                psE[:, :].rearrange("p (a b) -> p a b", b=128)[:, :, 0:1]),
        rd=[psE[:, :]], wr=[wsm[:, 0:8]], tag="ev_wch")
    if KPHASE == "p_t1":
        S.final_barrier()
        S.emit_all()
        return nc

    tr_group([psE[0:64, 32:64]], [pPT[64:96, 0:64]], idbf[:, :], "tr_om")
    dve(lambda e: e.tensor_copy(wsm[0:64, 8:40], psE[0:64, 32:64]),
        rd=[psE[0:64, 32:64]], wr=[wsm[0:64, 8:40]], tag="ev_om")
    if KPHASE == "p_t2":
        S.final_barrier()
        S.emit_all()
        return nc

    tr_group([psE[:, 64 + dc * 16: 64 + (dc + 1) * 16] for dc in range(8)],
             [pPT[32:48, dc * 128:(dc + 1) * 128] for dc in range(8)], idbf[:, :], "tr_gw")
    dve(lambda e: e.tensor_copy(wsm[:, 232:360], psE[:, 64:192]),
        rd=[psE[:, 64:192]], wr=[wsm[:, 232:360]], tag="ev_gw")
    if KPHASE == "p_t3":
        S.final_barrier()
        S.emit_all()
        return nc

    tr_group([psF[:, 0:8]], [scr8[32:40, 0:128]], idf[:, :], "tr_sb1")
    dve(lambda e: e.tensor_copy(bsb1[:, :], psF[:, 0:8]), rd=[psF[:, 0:8]], wr=[bsb1[:, :]], tag="ev_sb1")
    tr_group([psF[:, 8:16]], [scr8[64:72, 0:128]], idf[:, :], "tr_sb3")
    dve(lambda e: e.tensor_copy(bsb3[:, :], psF[:, 8:16]), rd=[psF[:, 8:16]], wr=[bsb3[:, :]], tag="ev_sb3")
    for dstT, tg, c0 in [(bb1, "b1", 0), (bb3, "b3", 512)]:
        nat = scr8[0:16, c0:c0 + 512]
        tr_group([psF[:, 16 + fc * 16: 16 + (fc + 1) * 16] for fc in range(4)],
                 [nat[0:16, fc * 128:(fc + 1) * 128] for fc in range(4)], idf[:, :], f"tr_{tg}")
        dve(lambda e, d=dstT: e.tensor_copy(d[:, :], psF[:, 16:80]),
            rd=[psF[:, 16:80]], wr=[dstT[:, :]], tag=f"ev_{tg}")

    tr_group([psE[:, 192:256]], [pPT[0:64, 1024:1152]], idbf[:, :], "tr_wc0")
    dve(lambda e: e.tensor_copy(wsm[:, 40:104], psE[:, 192:256]),
        rd=[psE[:, 192:256]], wr=[wsm[:, 40:104]], tag="ev_wc0")
    dve(lambda e: e.tensor_scalar_mul(wsm[0:64, 40:104], wsm[0:64, 40:104],
                                      MAGS[0] * RFF_SCALE),
        rd=[wsm[0:64, 40:104]], wr=[wsm[0:64, 40:104]], tag="wc0_scale")
    tr_group([psE[:, 256:320]], [pPT[0:64, 1152:1280]], idbf[:, :], "tr_wc1")
    dve(lambda e: e.tensor_copy(wsm[:, 104:168], psE[:, 256:320]),
        rd=[psE[:, 256:320]], wr=[wsm[:, 104:168]], tag="ev_wc1")
    tr_group([psE[0:64, 320:384]], [pPT[0:64, 1280:1344]], idbf[:, :], "tr_wc2")
    dve(lambda e: e.tensor_copy(wsm[0:64, 168:232], psE[0:64, 320:384]),
        rd=[psE[0:64, 320:384]], wr=[wsm[0:64, 168:232]], tag="ev_wc2")
    if KPHASE == "p_tr":
        S.final_barrier()
        S.emit_all()
        return nc


    tr_group([psE[:, dc * 64:(dc + 1) * 64] for dc in range(NDC)],
             [hbuf[0:64, 4096 + dc * 128: 4096 + (dc + 1) * 128] for dc in range(NDC)],
             idbf[:, :], "tr_wsh")
    dve(lambda e: e.tensor_copy(warena[:, WSHT:WSHT + 512], psE[:, 0:512]),
        rd=[psE[:, 0:512]], wr=[warena[:, WSHT:WSHT + 512]], tag="ev_wsh")
    dma("pool", pPT[0:1, 0:1024], W["L0_attn_b_v"][None, :], tag="bv")
    ddump("d_wsm", wsm[:, :], q="pool")
    for dc in range(NDC):
        dma("sp", warena[:, WVT + dc * 1024: WVT + (dc + 1) * 1024],
            wv_bf[0:D, dc * 128:(dc + 1) * 128], tag=f"wvT{dc}", transpose=True)
    S.add_dma("pool", lambda e: e.dma_start(out=wo_bf, in_=W["L0_attn_W_out"]),
              rd=[W["L0_attn_W_out"], warena[:, WVT + 7 * 1024: WVT + 8 * 1024]],
              wr=[wo_bf], tag="cast_wo")
    for e_ in range(4):
        dma("pool", w1_bf[e_], W["L0_ffn_w1"][e_], tag=f"cw1_{e_}")
        dma("pool", w3_bf[e_], W["L0_ffn_w3"][e_], tag=f"cw3_{e_}")
        dma("pool", w2_bf[e_], W["L0_ffn_w2"][e_], tag=f"cw2_{e_}")
    for dc in range(NDC):
        dma("sp", warena[:, WOT + dc * 1024: WOT + (dc + 1) * 1024],
            wo_bf[0:D, dc * 128:(dc + 1) * 128], tag=f"woT{dc}", transpose=True)
    dma("sp", cmask[:, :], cmask_d, tag="cmask")
    if KPHASE == "p_b":
        S.final_barrier()
        S.emit_all()
        return nc


    if KPHASE == "prologue":
        S.final_barrier()
        S.emit_all()
        return nc
    def lnT(dc, c0, c1):
        return fmA[:, dc * N + c0: dc * N + c1]

    fmA_3d = fmA[:, :].rearrange("p (c t) -> p c t", t=N)

    def ln_stats(tt, tag):
        par = tt % 2
        bn = bno[:, par * 12:(par + 1) * 12]
        ag = aggo[:, par * 2:(par + 1) * 2]
        lc = lncol[:, par * 2:(par + 1) * 2]
        stg = stage if par == 0 else stage2
        xt = xres[:, tt * D:(tt + 1) * D]
        dve(lambda e, xt=xt, bn=bn: e.bn_stats(bn[:, 0:6], xt[:, 0:512]),
            rd=[xt[:, 0:512]], wr=[bn[:, 0:6]], tag=f"bna_{tag}")
        dve(lambda e, xt=xt, bn=bn: e.bn_stats(bn[:, 6:12], xt[:, 512:1024]),
            rd=[xt[:, 512:1024]], wr=[bn[:, 6:12]], tag=f"bnb_{tag}")
        dve(lambda e, bn=bn, ag=ag: e.bn_aggr(ag, bn), rd=[bn], wr=[ag], tag=f"bag_{tag}")
        act(lambda e, ag=ag, lc=lc: e.activation(lc[:, 0:1], ag[:, 1:2], AF.Ln, bias=1e-5),
            rd=[ag[:, 1:2]], wr=[lc[:, 0:1]], tag=f"lnv_{tag}")
        act(lambda e, lc=lc: e.activation(lc[:, 1:2], lc[:, 0:1], AF.Exp, scale=-0.5),
            rd=[lc[:, 0:1]], wr=[lc[:, 1:2]], tag=f"rstd_{tag}")
        dve(lambda e, xt=xt, ag=ag, lc=lc, stg=stg: e.tensor_scalar(
            stg[:, :], xt, ag[:, 0:1], lc[:, 1:2], ALU.subtract, ALU.mult),
            rd=[xt, ag[:, 0:1], lc[:, 1:2]], wr=[stg[:, :]], tag=f"lnm_{tag}")

    def ln_fin(tt, tag):
        par = tt % 2
        stg = stage if par == 0 else stage2
        tr_group([psE[:, dc * 128:(dc + 1) * 128] for dc in range(8)],
                 [stg[:, dc * 128:(dc + 1) * 128] for dc in range(8)], idbf[:, :], f"lntr_{tag}")
        dst = fmA_3d[:, :, tt * 128:(tt + 1) * 128]
        act(lambda e, dst=dst: e.copy(dst, psE[:, :].rearrange("p (c t) -> p c t", t=128)),
            rd=[psE[:, :]],
            wr=[fmA[:, dc * N + tt * 128: dc * N + (tt + 1) * 128] for dc in range(NDC)],
            tag=f"lnev_{tag}")

    def layernorm(tt, tag):
        ln_stats(tt, tag)
        ln_fin(tt, tag)

    brow0 = pPT[0:1, 0:1024]
    def vproj(tt, bk=None):
        pc, pd = bk if bk else ((psC, psD) if tt % 2 == 0 else (psF, psH))
        lhsts = [lnT(dc, tt * 128, (tt + 1) * 128) for dc in range(NDC)]
        rhs_c = [warena[:, WVT + dc * 1024: WVT + dc * 1024 + 512] for dc in range(NDC)]
        rhs_d = [warena[:, WVT + dc * 1024 + 512: WVT + (dc + 1) * 1024] for dc in range(NDC)]
        mm_group2(pc, pd, lhsts, rhs_c, rhs_d, f"vp_{tt}",
                  pre_c=[(ones1[0:1, :], brow0[0:1, 0:512])],
                  pre_d=[(ones1[0:1, :], brow0[0:1, 512:1024])])
        act(lambda e, tt=tt, pc=pc: e.copy(fmB[:, tt * D: tt * D + 512], pc[:, :]),
            rd=[pc[:, :]], wr=[fmB[:, tt * D: tt * D + 512]], tag=f"vev0_{tt}")
        act(lambda e, tt=tt, pd=pd: e.copy(fmB[:, tt * D + 512: (tt + 1) * D], pd[:, :]),
            rd=[pd[:, :]], wr=[fmB[:, tt * D + 512: (tt + 1) * D]], tag=f"vev1_{tt}")

    shptr = [0]

    def shell_mm(tt):
        par = tt % 2
        ps = psA if par == 0 else psB
        shm = shtm[:, par * 64:(par + 1) * 64] if False else shtm
        shs = shst[:, par * 64:(par + 1) * 64] if False else shst
        pairs = [(lnT(dc, tt * 128, (tt + 1) * 128), warena[:, WSHT + dc * 64: WSHT + (dc + 1) * 64])
                 for dc in range(NDC)]
        mm_group(ps[:, 0:64], pairs, f"sh_{tt}")
        shm = shtm2 if par else shtm
        shs = shst2 if par else shst
        c9 = cols[:, 9:10] if par == 0 else lncol[:, 2:3]
        c10 = cols[:, 10:11] if par == 0 else lncol[:, 3:4]
        dve(lambda e, ps=ps, shm=shm: e.tensor_tensor(shm[:, :], ps[:, 0:64], bshb[:, :], ALU.add),
            rd=[ps[:, 0:64], bshb[:, :]], wr=[shm[:, :]], tag=f"shb_{tt}")
        act(lambda e, shm=shm, c9=c9: e.activation(sqs[:, :], shm[:, :], AF.Square, accum_out=c9),
            rd=[shm[:, :]], wr=[sqs[:, :], c9], tag=f"shsq_{tt}")
        act(lambda e, c9=c9, c10=c10: e.activation(c10, c9, AF.Ln),
            rd=[c9], wr=[c10], tag=f"shln_{tt}")
        act(lambda e, c10=c10: e.activation(c10, c10, AF.Exp, scale=0.5),
            rd=[c10], wr=[c10], tag=f"shex_{tt}")
        dve(lambda e, c10=c10: e.tensor_scalar_max(c10, c10, 1e-12),
            rd=[c10], wr=[c10], tag=f"shmx_{tt}")
        dve(lambda e, c9=c9, c10=c10: e.reciprocal(c9, c10),
            rd=[c10], wr=[c9], tag=f"shrc_{tt}")
        dve(lambda e, shm=shm, c9=c9: e.tensor_scalar_mul(shm[:, :], shm[:, :], c9),
            rd=[shm[:, :], c9], wr=[shm[:, :]], tag=f"shmul_{tt}")

    def shell_fin(tt):
        par = tt % 2
        shm = shtm2 if par else shtm
        tr_group([psG[0:64, 0:128]], [shm[:, :]], idf[:, :], f"shtr_{tt}")
        dve(lambda e, tt=tt: e.tensor_copy(hbuf[0:64, tt * 128:(tt + 1) * 128], psG[0:64, 0:128]),
            rd=[psG[0:64, 0:128]], wr=[hbuf[0:64, tt * 128:(tt + 1) * 128]], tag=f"shev_{tt}")

    def q_chunk(tc):
        qb = psF if tc % 2 == 0 else psC
        pairs = [(wsm[:, dc:dc + 1], lnT(dc, tc * 512, (tc + 1) * 512)) for dc in range(NDC)]
        mm_group(qb[0:1, 0:512], pairs, f"q_{tc}")
        act(lambda e, tc=tc, qb=qb: e.activation(scr8[0:1, tc * 512:(tc + 1) * 512], qb[0:1, 0:512],
                                                 AF.Tanh, bias=cols[0:1, 0:1]),
            rd=[qb[0:1, 0:512], cols[0:1, 0:1]], wr=[scr8[0:1, tc * 512:(tc + 1) * 512]], tag=f"qt_{tc}")
        dma("act", rrow_dr[0:1, tc * 512:(tc + 1) * 512], scr8[0:1, tc * 512:(tc + 1) * 512],
            tag=f"qst_{tc}")

    for tt in range(NTT + 3):
        if 1 <= tt <= NTT:
            ln_fin(tt - 1, f"l1_{tt - 1}")
        if tt < NTT:
            ln_stats(tt, f"l1_{tt}")
        if 2 <= tt <= NTT + 1:
            if 8 <= tt:
                vproj(tt - 8)
            shell_mm(tt - 2)
        if 3 <= tt <= NTT + 2:
            shell_fin(tt - 3)
        if 14 <= tt <= 16:
            q_chunk(tt - 14)
        if tt == 18:
            q_chunk(3)

    if KPHASE == "ln1":
        S.final_barrier()
        S.emit_all()
        return nc
    ddump("d_lnt", fmA[:, 0:2048], q="pool")

    if KPHASE == "vproj":
        S.final_barrier()
        S.emit_all()
        return nc
    QE, KE, PHIQ, PHIK = 6144, 2048, 0, 4096
    BIGC = 12582912.0
    PBANK = [psF, psH]
    dve(lambda e: e.tensor_scalar_add(cols[0:32, 2:3], cols[0:32, 2:3], math.pi / 2),
        rd=[cols[0:32, 2:3]], wr=[cols[0:32, 2:3]], tag="rffpi2")

    def proj_mm(tc):
        pairs = [(wsm[0:64, 8:40], hbuf[0:64, tc * 512:(tc + 1) * 512])]
        mm_group(psG[0:32, 0:512], pairs, f"proj_{tc}")
        mm_group(psG[32:64, 0:512], pairs, f"projb_{tc}")
        pb = PBANK[tc % 2][0:64, 0:512]
        dve(lambda e, pb=pb: e.tensor_scalar(pb, psG[0:64, 0:512], cols[0:64, 2:3],
                                             64.0 * TWO_PI, ALU.add, ALU.add),
            rd=[psG[0:64, 0:512], cols[0:64, 2:3]], wr=[pb], tag=f"prb_{tc}")

    def redsin64(tc):
        src = PBANK[tc % 2][0:64, 0:512]
        kf = scr8[0:64, (tc % 2) * 512:(tc % 2) * 512 + 512]
        tbuf = (s1a if tc % 2 == 0 else s1c)[0:64, 0:512]
        dst = hbuf[0:64, tc * 512:(tc + 1) * 512]
        act(lambda e, kf=kf, src=src: e.activation(kf, src, AF.Copy, bias=BIGC,
                                                   scale=1.0 / TWO_PI),
            rd=[src], wr=[kf], tag=f"k_{tc}")
        act(lambda e, kf=kf: e.activation(kf, kf, AF.Copy, bias=-BIGC),
            rd=[kf], wr=[kf], tag=f"ks_{tc}")
        dve(lambda e, kf=kf, src=src, tbuf=tbuf: e.scalar_tensor_tensor(
            tbuf, kf, -C1, src, ALU.mult, ALU.add),
            rd=[kf, src], wr=[tbuf], tag=f"t1_{tc}")
        dve(lambda e, kf=kf, tbuf=tbuf: e.scalar_tensor_tensor(
            tbuf, kf, -C2, tbuf, ALU.mult, ALU.add),
            rd=[kf, tbuf], wr=[tbuf], tag=f"t2_{tc}")
        dve(lambda e, tbuf=tbuf: e.tensor_scalar(tbuf, tbuf, PI_LO, -PI_LO, ALU.min, ALU.max),
            rd=[tbuf], wr=[tbuf], tag=f"tcl_{tc}")
        act(lambda e, dst=dst, tbuf=tbuf: e.activation(dst, tbuf, AF.Sin),
            rd=[tbuf], wr=[dst], tag=f"sin_{tc}")

    def tp_half(hh):
        c0, c1 = hh * 1024, (hh + 1) * 1024
        qbA = scr8[0:64, 0:1024]
        qbB = scr8[0:64, 1024:2048]
        ph = hbuf[0:64, c0:c1]
        dma("act", qbA, rrow_dr[0:1, c0:c1].partition_broadcast(64), tag=f"qbc{hh}")
        act(lambda e, qbA=qbA, qbB=qbB: e.activation(qbB, qbA, AF.Square),
            rd=[qbA], wr=[qbB], tag=f"pw2_{hh}")
        dve(lambda e, qbA=qbA, ph=ph, c0=c0: e.scalar_tensor_tensor(
            hbuf[64:128, c0:c0 + 1024], qbA, MAGS[1] * RFF_SCALE, ph, ALU.mult, ALU.mult),
            rd=[qbA, ph], wr=[hbuf[64:128, c0:c0 + 1024]], tag=f"tp1_{hh}")
        dve(lambda e, qbB=qbB, ph=ph, c0=c0: e.scalar_tensor_tensor(
            hbuf[0:64, 2048 + c0: 2048 + c0 + 1024], qbB, MAGS[2] * RFF_SCALE, ph,
            ALU.mult, ALU.mult),
            rd=[qbB, ph], wr=[hbuf[0:64, 2048 + c0: 2048 + c0 + 1024]], tag=f"tp2_{hh}")
        dve(lambda e, qbA=qbA, qbB=qbB: e.tensor_tensor(qbA, qbA, qbB, ALU.mult),
            rd=[qbA, qbB], wr=[qbA], tag=f"pw3_{hh}")
        dve(lambda e, qbA=qbA, ph=ph, c0=c0: e.scalar_tensor_tensor(
            hbuf[64:128, 2048 + c0: 2048 + c0 + 1024], qbA, MAGS[3] * RFF_SCALE, ph,
            ALU.mult, ALU.mult),
            rd=[qbA, ph], wr=[hbuf[64:128, 2048 + c0: 2048 + c0 + 1024]], tag=f"tp3_{hh}")
        act(lambda e, qbB=qbB: e.activation(qbB, qbB, AF.Square),
            rd=[qbB], wr=[qbB], tag=f"pw4_{hh}")
        dve(lambda e, qbB=qbB, ph=ph, c0=c0: e.scalar_tensor_tensor(
            hbuf[0:64, 4096 + c0: 4096 + c0 + 1024], qbB, MAGS[4] * RFF_SCALE, ph,
            ALU.mult, ALU.mult),
            rd=[qbB, ph], wr=[hbuf[0:64, 4096 + c0: 4096 + c0 + 1024]], tag=f"tp4_{hh}")

    def phif(tc):
        ps = psA if tc % 2 == 0 else psB
        pairs = [(wsm[:, 40:104], hbuf[:, tc * 512:(tc + 1) * 512]),
                 (wsm[:, 104:168], hbuf[:, 2048 + tc * 512: 2048 + (tc + 1) * 512]),
                 (wsm[0:64, 168:232], hbuf[0:64, 4096 + tc * 512: 4096 + (tc + 1) * 512])]
        mm_group(ps[0:64, 0:512], pairs, f"phi_{tc}")
        dma("act", hbuf[0:64, KE + tc * 512: KE + (tc + 1) * 512],
            kexp_d[0:1, tc * 512:(tc + 1) * 512].partition_broadcast(64), tag=f"kexp_{tc}")
        dve(lambda e, ps=ps, tc=tc: e.tensor_tensor(
            hbuf[0:64, PHIQ + tc * 512: PHIQ + (tc + 1) * 512], ps[0:64, 0:512],
            hbuf[0:64, QE + tc * 512: QE + (tc + 1) * 512], ALU.mult),
            rd=[ps[0:64, 0:512], hbuf[0:64, QE + tc * 512: QE + (tc + 1) * 512]],
            wr=[hbuf[0:64, PHIQ + tc * 512: PHIQ + (tc + 1) * 512]], tag=f"phiq_{tc}")
        dve(lambda e, ps=ps, tc=tc: e.tensor_tensor(
            hbuf[0:64, PHIK + tc * 512: PHIK + (tc + 1) * 512], ps[0:64, 0:512],
            hbuf[0:64, KE + tc * 512: KE + (tc + 1) * 512], ALU.mult),
            rd=[ps[0:64, 0:512], hbuf[0:64, KE + tc * 512: KE + (tc + 1) * 512]],
            wr=[hbuf[0:64, PHIK + tc * 512: PHIK + (tc + 1) * 512]], tag=f"phik_{tc}")

    proj_mm(0)
    redsin64(0)
    vproj(10, (psC, psD))
    proj_mm(1)
    redsin64(1)
    vproj(11, (psF, psH))
    tp_half(0)
    vproj(12, (psC, psD))
    phif(0)
    phif(1)
    proj_mm(2)
    redsin64(2)
    vproj(13, (psF, psH))
    proj_mm(3)
    redsin64(3)
    vproj(14, (psC, psD))
    vproj(15, (psF, psH))
    tp_half(1)
    phif(2)
    phif(3)
    ddump("d_phif", hbuf[:, 0:2048], q="pool")

    act(lambda e: e.activation(cols[0:1, 5:6], cols[0:1, 4:5], AF.Relu),
        rd=[cols[0:1, 4:5]], wr=[cols[0:1, 5:6]], tag="sp_relu")
    act(lambda e: e.activation(cols[0:1, 6:7], cols[0:1, 4:5], AF.Abs),
        rd=[cols[0:1, 4:5]], wr=[cols[0:1, 6:7]], tag="sp_abs")
    act(lambda e: e.activation(cols[0:1, 7:8], cols[0:1, 6:7], AF.Exp, scale=-1.0),
        rd=[cols[0:1, 6:7]], wr=[cols[0:1, 7:8]], tag="sp_exp")
    act(lambda e: e.activation(cols[0:1, 6:7], cols[0:1, 7:8], AF.Ln, bias=1.0),
        rd=[cols[0:1, 7:8]], wr=[cols[0:1, 6:7]], tag="sp_ln")
    dve(lambda e: e.tensor_tensor(cols[0:1, 5:6], cols[0:1, 5:6], cols[0:1, 6:7], ALU.add),
        rd=[cols[0:1, 5:6], cols[0:1, 6:7]], wr=[cols[0:1, 5:6]], tag="sp_add")
    dve(lambda e: e.tensor_scalar_add(cols[0:1, 5:6], cols[0:1, 5:6], 0.1),
        rd=[cols[0:1, 5:6]], wr=[cols[0:1, 5:6]], tag="sp_add2")
    dve(lambda e: e.reciprocal(cols[0:1, 8:9], cols[0:1, 5:6]),
        rd=[cols[0:1, 5:6]], wr=[cols[0:1, 8:9]], tag="sp_recip")
    onef = s1c[0:1, 0:128]
    dve(lambda e: e.memset(onef, 1.0), rd=[], wr=[onef], tag="onesf")
    mm_group(psF[:, 80:81], [(onef, cols[0:1, 8:9])], "bc_invt")
    dve(lambda e: e.tensor_copy(cols[:, 3:4], psF[:, 80:81]),
        rd=[psF[:, 80:81]], wr=[cols[:, 3:4]], tag="ev_invt")

    if KPHASE == "phi":
        S.final_barrier()
        S.emit_all()
        return nc
    def gate_mm(tt):
        ps_g = psF
        pairs = [(lnT(dc, tt * 128, (tt + 1) * 128), wsm[:, 232 + dc * 16: 232 + (dc + 1) * 16])
                 for dc in range(NDC)]
        mm_group(ps_g[:, 100:116], pairs, f"g_{tt}")
        dve(lambda e, tt=tt, ps_g=ps_g: e.tensor_copy(s1a[:, tt * 16:(tt + 1) * 16], ps_g[:, 100:116]),
            rd=[ps_g[:, 100:116]], wr=[s1a[:, tt * 16:(tt + 1) * 16]], tag=f"gev_{tt}")

    brow_o = hbuf[0:1, 3072:4096]
    dma("pool", brow_o, W["L0_attn_b_out"][None, :], tag="bo")

    def out_proj(tt):
        lhsts = [lnT(dc, tt * 128, (tt + 1) * 128) for dc in range(NDC)]
        rhs_c = [warena[:, WOT + dc * 1024: WOT + dc * 1024 + 512] for dc in range(NDC)]
        rhs_d = [warena[:, WOT + dc * 1024 + 512: WOT + (dc + 1) * 1024] for dc in range(NDC)]
        mm_group2(psC, psD, lhsts, rhs_c, rhs_d, f"op_{tt}",
                  pre_c=[(ones1[0:1, :], brow_o[0:1, 0:512])],
                  pre_d=[(ones1[0:1, :], brow_o[0:1, 512:1024])])
        xs0 = xres[:, tt * D: tt * D + 512]
        xs1 = xres[:, tt * D + 512: (tt + 1) * D]
        dve(lambda e, xs0=xs0: e.tensor_tensor(xs0, psC[:, :], xs0, ALU.add),
            rd=[psC[:, :], xs0], wr=[xs0], tag=f"orev0_{tt}")
        dve(lambda e, xs1=xs1: e.tensor_tensor(xs1, psD[:, :], xs1, ALU.add),
            rd=[psD[:, :], xs1], wr=[xs1], tag=f"orev1_{tt}")

    def s_mms(qi):
        lq = hbuf[0:64, PHIQ + qi * 128: PHIQ + (qi + 1) * 128]
        nf = qi // 4
        SROT = (psA, psB, psH)
        for b in range(nf):
            ps = SROT[b % 3]
            mm_group(ps[:, 0:512], [(lq, hbuf[0:64, PHIK + b * 512: PHIK + (b + 1) * 512])], f"s{qi}_{b}")
            dve(lambda e, ps=ps, b=b, qi=qi: e.tensor_copy(scr8[:, b * 512:(b + 1) * 512], ps[:, 0:512]),
                rd=[ps[:, 0:512]], wr=[scr8[:, b * 512:(b + 1) * 512]], tag=f"sev{qi}_{b}")
        rem = qi - 4 * nf
        if rem:
            ps = SROT[nf % 3]
            mm_group(ps[:, 0:rem * 128],
                     [(lq, hbuf[0:64, PHIK + 4 * nf * 128: PHIK + qi * 128])], f"sr{qi}")
            dve(lambda e, ps=ps, nf=nf, rem=rem, qi=qi: e.tensor_copy(
                scr8[:, 4 * nf * 128: qi * 128], ps[:, 0:rem * 128]),
                rd=[ps[:, 0:rem * 128]], wr=[scr8[:, 4 * nf * 128: qi * 128]], tag=f"srev{qi}")
        mm_group(psG[:, 0:128], [(lq, hbuf[0:64, PHIK + qi * 128: PHIK + (qi + 1) * 128])], f"sd{qi}")
        dve(lambda e, qi=qi: e.tensor_tensor(scr8[:, qi * 128:(qi + 1) * 128],
                                             psG[:, 0:128], cmask[:, :], ALU.add),
            rd=[psG[:, 0:128], cmask[:, :]], wr=[scr8[:, qi * 128:(qi + 1) * 128]], tag=f"sdev{qi}")

    s_mms(0)
    for qi in range(NTT):
        Wd = (qi + 1) * 128
        nf = qi // 4
        for b in range(nf):
            dve(lambda e, b=b: e.reduce_max(cols[:, 5 + b: 6 + b], scr8[:, b * 512:(b + 1) * 512], AX.X),
                rd=[scr8[:, b * 512:(b + 1) * 512]], wr=[cols[:, 5 + b: 6 + b]], tag=f"spm{qi}_{b}")
        dve(lambda e, nf=nf, Wd=Wd: e.reduce_max(cols[:, 5 + nf: 6 + nf],
                                                 scr8[:, 4 * nf * 128: Wd], AX.X),
            rd=[scr8[:, 4 * nf * 128: Wd]], wr=[cols[:, 5 + nf: 6 + nf]], tag=f"stm{qi}")
        dve(lambda e, nf=nf: e.reduce_max(cols[:, 9:10], cols[:, 5: 6 + nf], AX.X),
            rd=[cols[:, 5: 6 + nf]], wr=[cols[:, 9:10]], tag=f"smx{qi}")
        dve(lambda e: e.tensor_scalar(cols[:, 10:11], cols[:, 9:10], cols[:, 3:4], -1.0,
                                      ALU.mult, ALU.mult),
            rd=[cols[:, 9:10], cols[:, 3:4]], wr=[cols[:, 10:11]], tag=f"snb{qi}")
        if qi > 0:
            out_proj(qi - 1)
        if qi == 3:
            ddump("d_srow", scr8[:, :])
        nck = (Wd + 511) // 512
        for b in range(nck):
            c0, c1 = b * 512, min((b + 1) * 512, Wd)
            act(lambda e, c0=c0, c1=c1, b=b: e.activation(pPT[:, c0:c1], scr8[:, c0:c1], AF.Exp,
                                                          bias=cols[:, 10:11], scale=cols[:, 3:4],
                                                          accum_out=cols[:, 5 + b: 6 + b]),
                rd=[scr8[:, c0:c1], cols[:, 10:11], cols[:, 3:4]],
                wr=[pPT[:, c0:c1], cols[:, 5 + b: 6 + b]], tag=f"sexp{qi}_{b}")
            npt = (c1 - c0 + 127) // 128
            tr_group([psE[:, i * 128:(i + 1) * 128] for i in range(npt)],
                     [pPT[:, c0 + i * 128: c0 + (i + 1) * 128] for i in range(npt)],
                     idbf[:, :], f"ptr{qi}_{b}")
            dve(lambda e, c0=c0, c1=c1: e.tensor_copy(pPT[:, c0:c1], psE[:, 0:c1 - c0]),
                rd=[psE[:, 0:c1 - c0]], wr=[pPT[:, c0:c1]], tag=f"ptev{qi}_{b}")
        if qi + 1 < NTT:
            s_mms(qi + 1)
        dve(lambda e, nck=nck: e.reduce_sum(cols[:, 11:12], cols[:, 5:5 + nck], AX.X),
            rd=[cols[:, 5:5 + nck]], wr=[cols[:, 11:12]], tag=f"ssum{qi}")
        dve(lambda e: e.reciprocal(cols[:, 12:13], cols[:, 11:12]),
            rd=[cols[:, 11:12]], wr=[cols[:, 12:13]], tag=f"srs{qi}")
        for b0 in range(0, qi + 1, 4):
            b1 = min(b0 + 4, qi + 1)
            lhsts = [pPT[:, kc * 128:(kc + 1) * 128] for kc in range(b0, b1)]
            rhs_c = [fmB[:, kc * D: kc * D + 512] for kc in range(b0, b1)]
            rhs_d = [fmB[:, kc * D + 512: (kc + 1) * D] for kc in range(b0, b1)]
            mm_group2(psC, psD, lhsts, rhs_c, rhs_d, f"pv{qi}_{b0}",
                      cont=(b0 > 0), open_end=(b1 < qi + 1))
        dve(lambda e: e.tensor_scalar_mul(hbuf[:, 2048:2560], psC[:, :], cols[:, 12:13]),
            rd=[psC[:, :], cols[:, 12:13]], wr=[hbuf[:, 2048:2560]], tag=f"oev0{qi}")
        dve(lambda e: e.tensor_scalar_mul(hbuf[:, 2560:3072], psD[:, :], cols[:, 12:13]),
            rd=[psD[:, :], cols[:, 12:13]], wr=[hbuf[:, 2560:3072]], tag=f"oev1{qi}")
        tr_group([psE[:, dc * 128:(dc + 1) * 128] for dc in range(8)],
                 [hbuf[:, 2048 + dc * 128: 2048 + (dc + 1) * 128] for dc in range(8)],
                 idbf[:, :], f"otr{qi}")
        dst = fmA_3d[:, :, qi * 128:(qi + 1) * 128]
        dve(lambda e, dst=dst: e.tensor_copy(dst, psE[:, :].rearrange("p (c t) -> p c t", t=128)),
            rd=[psE[:, :]],
            wr=[fmA[:, dc * N + qi * 128: dc * N + (qi + 1) * 128] for dc in range(NDC)],
            tag=f"otev{qi}")
        if qi >= 2:
            ln_fin(qi - 2, f"l2_{qi - 2}")
            gate_mm(qi - 2)
        if qi > 0:
            ln_stats(qi - 1, f"l2_{qi - 1}")

    if KPHASE == "sloop":
        S.final_barrier()
        S.emit_all()
        return nc
    out_proj(NTT - 1)
    ln_stats(NTT - 1, f"l2_{NTT - 1}")
    ln_fin(NTT - 2, f"l2_{NTT - 2}")
    gate_mm(NTT - 2)
    ln_fin(NTT - 1, f"l2_{NTT - 1}")
    gate_mm(NTT - 1)
    ddump("d_xattn", xres[:, 0:1024])

    if KPHASE == "oproj":
        S.final_barrier()
        S.emit_all()
        return nc

    if KPHASE == "ln2":
        S.final_barrier()
        S.emit_all()
        return nc
    for e_ in range(4, NE):
        dma("pool", w1_bf[e_], W["L0_ffn_w1"][e_], tag=f"cw1_{e_}")
        dma("pool", w3_bf[e_], W["L0_ffn_w3"][e_], tag=f"cw3_{e_}")
        dma("pool", w2_bf[e_], W["L0_ffn_w2"][e_], tag=f"cw2_{e_}")
    dma("pool", sw1_bf, W["L0_ffn_sw1"], tag="c_sw1")
    dma("pool", sw3_bf, W["L0_ffn_sw3"], tag="c_sw3")
    dma("pool", sw2_bf, W["L0_ffn_sw2"], tag="c_sw2")

    W2S = 16384

    def ln2T(dc, c0, c1):
        return fmA[:, dc * N + c0: dc * N + c1]

    def load_expert_w13(e_):
        slot = e_ % 2
        W1S = slot * 8192
        W3S = slot * 8192 + 4096
        for dc in range(NDC):
            dma("sp", warena[:, W1S + dc * 512: W1S + (dc + 1) * 512],
                w1_bf[e_][0:DFF, dc * 128:(dc + 1) * 128], tag=f"w1T{e_}_{dc}", transpose=True)
        for dc in range(NDC):
            dma("sp", warena[:, W3S + dc * 512: W3S + (dc + 1) * 512],
                w3_bf[e_][0:DFF, dc * 128:(dc + 1) * 128], tag=f"w3T{e_}_{dc}", transpose=True)

    def load_expert_w2(e_):
        for fc in range(4):
            dma("sp", warena[:, W2S + fc * 1024: W2S + (fc + 1) * 1024],
                w2_bf[e_][0:D, fc * 128:(fc + 1) * 128], tag=f"w2T{e_}_{fc}", transpose=True)

    def h_phase_tc(e_, tc):
        slot = e_ % 2
        W1S = slot * 8192
        W3S = slot * 8192 + 4096
        if True:
            for fc in range(4):
                k = (tc * 4 + fc) % 2
                p1 = psA if k == 0 else psG
                p3 = psB if k == 0 else psH
                s1x = s1a if k == 0 else s1c
                mm_group(p1[:, :], [(warena[:, W1S + dc * 512 + fc * 128: W1S + dc * 512 + (fc + 1) * 128],
                                     ln2T(dc, tc * 512, (tc + 1) * 512)) for dc in range(NDC)],
                         f"h1_{e_}_{tc}_{fc}")
                mm_group(p3[:, :], [(warena[:, W3S + dc * 512 + fc * 128: W3S + dc * 512 + (fc + 1) * 128],
                                     ln2T(dc, tc * 512, (tc + 1) * 512)) for dc in range(NDC)],
                         f"h3_{e_}_{tc}_{fc}")
                bc1 = bb1[:, fc * 16 + e_: fc * 16 + e_ + 1]
                bc3 = bb3[:, fc * 16 + e_: fc * 16 + e_ + 1]
                act(lambda e, p1=p1, s1x=s1x, bc1=bc1: e.activation(s1x[:, :], p1[:, :], AF.Silu, bias=bc1),
                    rd=[p1[:, :], bc1], wr=[s1x[:, :]], tag=f"si_{e_}_{tc}_{fc}")
                hd = hbuf[:, fc * 2048 + tc * 512: fc * 2048 + (tc + 1) * 512]
                dve(lambda e, p3=p3, s1x=s1x, bc3=bc3, hd=hd: e.scalar_tensor_tensor(
                    hd, p3[:, :], bc3, s1x[:, :], ALU.add, ALU.mult),
                    rd=[p3[:, :], bc3, s1x[:, :]], wr=[hd], tag=f"hm_{e_}_{tc}_{fc}")
    def w2_phase(e_):
        for ts_ in range(NTT):
            pc_ = psC if ts_ % 2 == 0 else psF
            pd_ = psD if ts_ % 2 == 0 else psH
            lhsts = [hbuf[:, fc * 2048 + ts_ * 128: fc * 2048 + (ts_ + 1) * 128] for fc in range(4)]
            rhs_c = [warena[:, W2S + fc * 1024: W2S + fc * 1024 + 512] for fc in range(4)]
            rhs_d = [warena[:, W2S + fc * 1024 + 512: W2S + (fc + 1) * 1024] for fc in range(4)]
            mm_group2(pc_, pd_, lhsts, rhs_c, rhs_d, f"y_{e_}_{ts_}")
            gcol = gatew[:, ts_ * 16 + e_: ts_ * 16 + e_ + 1]
            xs0 = xres[:, ts_ * D: ts_ * D + 512]
            xs1 = xres[:, ts_ * D + 512: (ts_ + 1) * D]
            dve(lambda e, xs0=xs0, gcol=gcol, pc_=pc_: e.scalar_tensor_tensor(xs0, pc_[:, :], gcol, xs0,
                                                                              ALU.mult, ALU.add),
                rd=[pc_[:, :], gcol, xs0], wr=[xs0], tag=f"ye0_{e_}_{ts_}")
            dve(lambda e, xs1=xs1, gcol=gcol, pd_=pd_: e.scalar_tensor_tensor(xs1, pd_[:, :], gcol, xs1,
                                                                              ALU.mult, ALU.add),
                rd=[pd_[:, :], gcol, xs1], wr=[xs1], tag=f"ye1_{e_}_{ts_}")

    if KPHASE == "experts":
        S.final_barrier()
        S.emit_all()
        return nc
    load_expert_w13(0)
    load_expert_w2(0)


    logits = s1a[:, 0:256]
    ddump("d_logits", s1a[:, 0:256])
    act(lambda e: e.activation(s1c[:, 0:256], logits, AF.Sigmoid),
        rd=[logits], wr=[s1c[:, 0:256]], tag="gsig")
    v3 = lambda ap: ap.rearrange("p (a b) -> p a b", b=16)
    eb_b = ebias_bc[:, 0:16].unsqueeze(1).broadcast_to([128, 16, 16])
    dve(lambda e: e.tensor_tensor(v3(s1a[:, 256:512]), v3(logits), eb_b, ALU.add),
        rd=[logits, ebias_bc[:, 0:16]], wr=[s1a[:, 256:512]], tag="gbias")
    biased = s1a[:, 256:512]
    dve(lambda e: e.reduce_max(gstat[:, 0:16], v3(biased), AX.X),
        rd=[biased], wr=[gstat[:, 0:16]], tag="gm1")
    m1b = gstat[:, 0:16].unsqueeze(2).broadcast_to([128, 16, 16])
    dve(lambda e: e.tensor_tensor(v3(s1c[:, 256:512]), v3(biased), m1b, ALU.is_equal),
        rd=[biased, gstat[:, 0:16]], wr=[s1c[:, 256:512]], tag="gmk1")
    mask1 = s1c[:, 256:512]
    dve(lambda e: e.scalar_tensor_tensor(logits, mask1, -1e30, biased, ALU.mult, ALU.add),
        rd=[mask1, biased], wr=[logits], tag="gb2")
    b2t = logits
    dve(lambda e: e.reduce_max(gstat[:, 16:32], v3(b2t), AX.X),
        rd=[b2t], wr=[gstat[:, 16:32]], tag="gm2")
    m2b = gstat[:, 16:32].unsqueeze(2).broadcast_to([128, 16, 16])
    dve(lambda e: e.tensor_tensor(v3(biased), v3(b2t), m2b, ALU.is_equal),
        rd=[b2t, gstat[:, 16:32]], wr=[biased], tag="gmk2")
    mask2 = biased
    dve(lambda e: e.tensor_tensor(mask1, mask1, mask2, ALU.add),
        rd=[mask1, mask2], wr=[mask1], tag="gmsk")
    dve(lambda e: e.tensor_tensor(b2t, mask1, s1c[:, 0:256], ALU.mult),
        rd=[mask1, s1c[:, 0:256]], wr=[b2t], tag="gwsel")
    wsel = b2t
    dve(lambda e: e.reduce_sum(gstat[:, 32:48], v3(wsel), AX.X),
        rd=[wsel], wr=[gstat[:, 32:48]], tag="gden")
    dve(lambda e: e.tensor_scalar_add(gstat[:, 32:48], gstat[:, 32:48], 1e-10),
        rd=[gstat[:, 32:48]], wr=[gstat[:, 32:48]], tag="gden2")
    dve(lambda e: e.reciprocal(gstat[:, 48:64], gstat[:, 32:48]),
        rd=[gstat[:, 32:48]], wr=[gstat[:, 48:64]], tag="grd")
    rdb = gstat[:, 48:64].unsqueeze(2).broadcast_to([128, 16, 16])
    dve(lambda e: e.tensor_tensor(v3(gatew[:, 0:256]), v3(wsel), rdb, ALU.mult),
        rd=[wsel, gstat[:, 48:64]], wr=[gatew[:, 0:256]], tag="gw")
    ddump("d_gatew", gatew[:, 0:256])

    if KPHASE == "gate":
        S.final_barrier()
        S.emit_all()
        return nc
    for tc in range(4):
        h_phase_tc(0, tc)
    w2_phase(0)
    for e_ in range(1, NE):
        load_expert_w13(e_)
        load_expert_w2(e_)
        for tc in range(4):
            h_phase_tc(e_, tc)
        w2_phase(e_)

    for half in range(2):
        HB = half * 8192
        for dc in range(NDC):
            dma("sp", warena[:, HB + dc * 512: HB + (dc + 1) * 512],
                sw1_bf[half * 512:(half + 1) * 512, dc * 128:(dc + 1) * 128],
                tag=f"sw1T{half}_{dc}", transpose=True)
        for dc in range(NDC):
            dma("sp", warena[:, HB + 4096 + dc * 512: HB + 4096 + (dc + 1) * 512],
                sw3_bf[half * 512:(half + 1) * 512, dc * 128:(dc + 1) * 128],
                tag=f"sw3T{half}_{dc}", transpose=True)
        for tc in range(4):
            for fc in range(4):
                fcg = half * 4 + fc
                k = (tc * 4 + fc) % 2
                p1 = psA if k == 0 else psG
                p3 = psB if k == 0 else psH
                s1x = s1a if k == 0 else s1c
                mm_group(p1[:, :], [(warena[:, HB + dc * 512 + fc * 128: HB + dc * 512 + (fc + 1) * 128],
                                     ln2T(dc, tc * 512, (tc + 1) * 512)) for dc in range(NDC)],
                         f"sh1_{half}_{tc}_{fc}")
                mm_group(p3[:, :], [(warena[:, HB + 4096 + dc * 512 + fc * 128: HB + 4096 + dc * 512 + (fc + 1) * 128],
                                     ln2T(dc, tc * 512, (tc + 1) * 512)) for dc in range(NDC)],
                         f"sh3_{half}_{tc}_{fc}")
                bc1 = bsb1[:, fcg: fcg + 1]
                bc3 = bsb3[:, fcg: fcg + 1]
                act(lambda e, p1=p1, s1x=s1x, bc1=bc1: e.activation(s1x[:, :], p1[:, :], AF.Silu, bias=bc1),
                    rd=[p1[:, :], bc1], wr=[s1x[:, :]], tag=f"ssi_{half}_{tc}_{fc}")
                hd = fmB[:, fcg * 2048 + tc * 512: fcg * 2048 + (tc + 1) * 512]
                dve(lambda e, p3=p3, s1x=s1x, bc3=bc3, hd=hd: e.scalar_tensor_tensor(
                    hd, p3[:, :], bc3, s1x[:, :], ALU.add, ALU.mult),
                    rd=[p3[:, :], bc3, s1x[:, :]], wr=[hd], tag=f"shm_{half}_{tc}_{fc}")
    for fcg in range(8):
        dma("sp", warena[:, fcg * 1024:(fcg + 1) * 1024],
            sw2_bf[0:D, fcg * 128:(fcg + 1) * 128], tag=f"sw2T{fcg}", transpose=True)
    for b4 in range(4):
        tr_group([psF[0:16, i * 128:(i + 1) * 128] for i in range(4)],
                 [gatew[:, (b4 * 4 + i) * 16:(b4 * 4 + i + 1) * 16] for i in range(4)],
                 idf[:, :], f"gtr_{b4}")
        dve(lambda e, b4=b4: e.tensor_copy(gatewT[0:16, b4 * 512:(b4 + 1) * 512], psF[0:16, 0:512]),
            rd=[psF[0:16, 0:512]], wr=[gatewT[0:16, b4 * 512:(b4 + 1) * 512]], tag=f"gtev_{b4}")
    dma("pool", b2bf[:, :], W["L0_ffn_b2"], tag="b2bf")
    dma("pool", brow0, W["L0_ffn_sb2"][None, :], tag="sb2")
    for ts_ in range(NTT):
        lhsts = [fmB[:, fcg * 2048 + ts_ * 128: fcg * 2048 + (ts_ + 1) * 128] for fcg in range(8)]
        rhs_c = [warena[:, fcg * 1024: fcg * 1024 + 512] for fcg in range(8)]
        rhs_d = [warena[:, fcg * 1024 + 512:(fcg + 1) * 1024] for fcg in range(8)]
        gT = gatewT[0:16, ts_ * 128:(ts_ + 1) * 128]
        pc = [(gT, b2bf[0:16, 0:512]), (ones1[0:1, :], brow0[0:1, 0:512])]
        pd = [(gT, b2bf[0:16, 512:1024]), (ones1[0:1, :], brow0[0:1, 512:1024])]
        if KOPT == "nopre":
            pc, pd = [], []
        elif KOPT == "nob2":
            pc, pd = pc[1:], pd[1:]
        elif KOPT == "nosb2":
            pc, pd = pc[:1], pd[:1]
        pc_b = psC if ts_ % 2 == 0 else psF
        pd_b = psD if ts_ % 2 == 0 else psH
        mm_group2(pc_b, pd_b, lhsts, rhs_c, rhs_d, f"sy_{ts_}", pre_c=pc, pre_d=pd)
        xs0 = xres[:, ts_ * D: ts_ * D + 512]
        xs1 = xres[:, ts_ * D + 512: (ts_ + 1) * D]
        dve(lambda e, xs0=xs0, pc_b=pc_b: e.tensor_tensor(xs0, pc_b[:, :], xs0, ALU.add),
            rd=[pc_b[:, :], xs0], wr=[xs0], tag=f"sye0_{ts_}")
        dve(lambda e, xs1=xs1, pd_b=pd_b: e.tensor_tensor(xs1, pd_b[:, :], xs1, ALU.add),
            rd=[pd_b[:, :], xs1], wr=[xs1], tag=f"sye1_{ts_}")

    if KPHASE == "shared":
        S.final_barrier()
        S.emit_all()
        return nc
    for g in range(4):
        dst = out[g * 512:(g + 1) * 512, :].rearrange("(t p) d -> p t d", p=128)
        src = xres[:, g * 4096:(g + 1) * 4096].rearrange("p (t d) -> p t d", d=D)
        dma("act", dst, src, tag=f"xst{g}")

    S.final_barrier()
    cnt = {}
    for o in S.seq:
        cnt[o.eng] = cnt.get(o.eng, 0) + 1
    print("op counts:", cnt, " total", len(S.seq))
    S.emit_all()
    return nc


_NC_CACHE = {}


def kernel(**inputs):
    x = np.ascontiguousarray(np.asarray(inputs["L0_attn_x"], dtype=np.float32))
    if "nc" not in _NC_CACHE:
        _NC_CACHE["nc"] = build_nc()
    nc = _NC_CACHE["nc"]
    warr = {}
    for k, shp in WEIGHT_SHAPES.items():
        a = np.ascontiguousarray(np.asarray(inputs[k], dtype=np.float32)).reshape(shp)
        warr[k] = a
    in_maps = [dict(x_in=x[c], **warr) for c in range(8)]
    res = run_bass_kernel_spmd(nc, in_maps, core_ids=list(range(8)))
    kernel.last_results = res.results
    return np.stack([np.asarray(r["out"]) for r in res.results], axis=0)


if __name__ == "__main__":
    nc = build_nc()
    print("build ok")


# version 47
# speedup vs baseline: 1.0127x; 1.0127x over previous
import math
import os
import numpy as np
import ml_dtypes
import concourse.bass as bass
import concourse.mybir as mybir
from concourse.bass_utils import run_bass_kernel_spmd

F32 = mybir.dt.float32
BF16 = mybir.dt.bfloat16
I32 = mybir.dt.int32
AF = mybir.ActivationFunctionType
ALU = mybir.AluOpType
AX = mybir.AxisListType

N = 2048
D = 1024
NTT = 16
NDC = 8
DFF = 512
NE = 16
FDFF = 1024

RFF_SCALE = math.sqrt(2.0 / 64.0) * math.sqrt(math.exp(-0.5))
MAGS = [math.sqrt(abs((-1.0) ** m / math.factorial(m)) + 1e-10) for m in range(5)]
TWO_PI = 2.0 * math.pi
C1 = 6.28125
C2 = TWO_PI - C1
PI_LO = float(np.nextafter(np.float32(np.pi), np.float32(0.0)))
DECAY = 0.03

DEBUG = bool(int(os.environ.get("KDEBUG", "0")))
KPHASE = os.environ.get("KPHASE", "")
KOPT = os.environ.get("KOPT", "")

ENGN = ("pe", "act", "dve", "pool", "sp")
QUEUES = ("sp", "act", "pool")
NLANES = 8


class Op:
    __slots__ = ("eng", "emit", "waits", "inc", "vc", "idx", "tag", "comp")

    def __init__(self, eng, emit, tag):
        self.eng = eng
        self.emit = emit
        self.waits = []
        self.inc = None
        self.vc = {}
        self.idx = -1
        self.tag = tag
        self.comp = None


class Sched:
    def __init__(self, nc):
        self.nc = nc
        self.seq = []
        self.regions = {}
        self.esem = {e: nc.alloc_semaphore(f"es_{e}") for e in ("pe", "act", "dve", "pool")}
        self.ecount = {e: 0 for e in ("pe", "act", "dve", "pool")}
        self.lanes = {q: [nc.alloc_semaphore(f"dl_{q}{i}") for i in range(NLANES)] for q in QUEUES}
        self.lane_uses = {q: [0] * NLANES for q in QUEUES}
        self.lane_last = {q: [None] * NLANES for q in QUEUES}
        self.dma_idx = {q: 0 for q in QUEUES}
        self.evc = {e: {} for e in ENGN}

    def _reg(self, ap):
        if not hasattr(ap, "tensor"):
            ap = ap[tuple(slice(None) for _ in ap.shape)]
        t = ap.tensor
        dims = [list(d) for d in ap.ap]
        off = ap.offset
        assert isinstance(off, int), f"dynamic offset unsupported: {ap}"
        es = mybir.dt.size(ap.dtype)
        sp = str(ap.space).upper()
        if "DRAM" in sp or "HBM" in sp:
            ext = sum(abs(s) * (c - 1) for s, c in dims)
            return (t.name, off * es, (off + ext + 1) * es)
        if "PSUM" in sp:
            return (t.name, 0, 1 << 30)
        row = dims[0][0] if dims else 1
        ext = sum(abs(s) * (c - 1) for s, c in dims[1:]) if len(dims) > 1 else 0
        lo = off % row if row > 0 else off
        return (t.name, lo * es, (lo + ext + 1) * es)

    def _deps_update(self, op, reads, writes):
        deps = set()
        for (nm, lo, hi) in reads:
            recs = self.regions.get(nm)
            if recs:
                for r in recs:
                    if r[0] == "w" and r[1] < hi and lo < r[2]:
                        deps.add(r[3])
        for (nm, lo, hi) in writes:
            recs = self.regions.get(nm)
            if recs:
                for r in recs:
                    if r[1] < hi and lo < r[2]:
                        deps.add(r[3])
        for (nm, lo, hi) in reads:
            self.regions.setdefault(nm, []).append(("r", lo, hi, op))
        for (nm, lo, hi) in writes:
            recs = self.regions.setdefault(nm, [])
            recs[:] = [r for r in recs if not (lo <= r[1] and r[2] <= hi)]
            recs.append(("w", lo, hi, op))
        return deps

    def _resolve(self, op, deps):
        evc = self.evc[op.eng]
        pe_sem = self.esem["pe"]
        for d in sorted(deps, key=lambda o: o.idx):
            s, v = d.comp
            if evc.get(s, 0) >= v:
                continue
            if op.eng == "pe" and d.eng == "pe" and s is pe_sem:
                evc[s] = v
                for ks, kv in d.vc.items():
                    if evc.get(ks, 0) < kv:
                        evc[ks] = kv
                continue
            op.waits.append((s, v))
            evc[s] = v
            for ks, kv in d.vc.items():
                if evc.get(ks, 0) < kv:
                    evc[ks] = kv
        op.vc = dict(evc)
        if op.comp is not None:
            op.vc[op.comp[0]] = op.comp[1]

    def add(self, eng, emit, rd, wr, tag=""):
        op = Op(eng, emit, tag)
        op.idx = len(self.seq)
        reads = [self._reg(a) for a in rd]
        writes = [self._reg(a) for a in wr]
        deps = self._deps_update(op, reads, writes)
        self.ecount[eng] += 1
        op.comp = (self.esem[eng], self.ecount[eng])
        op.inc = (self.esem[eng], 1)
        self._resolve(op, deps)
        self.seq.append(op)
        return op

    def add_dma(self, queue, emit, rd, wr, tag=""):
        eng = queue
        op = Op(eng, emit, tag)
        op.idx = len(self.seq)
        reads = [self._reg(a) for a in rd]
        writes = [self._reg(a) for a in wr]
        deps = self._deps_update(op, reads, writes)
        lane = self.dma_idx[queue] % NLANES
        self.dma_idx[queue] += 1
        prev = self.lane_last[queue][lane]
        if prev is not None:
            deps.add(prev)
        self.lane_uses[queue][lane] += 1
        sem = self.lanes[queue][lane]
        op.comp = (sem, 16 * self.lane_uses[queue][lane])
        op.inc = (sem, 16)
        self.lane_last[queue][lane] = op
        self._resolve(op, deps)
        self.seq.append(op)
        return op

    def final_barrier(self):
        waits = []
        for q in QUEUES:
            for i in range(NLANES):
                if self.lane_uses[q][i]:
                    waits.append((self.lanes[q][i], 16 * self.lane_uses[q][i]))
        for e in ("pe", "act", "dve", "pool"):
            if self.ecount[e]:
                waits.append((self.esem[e], self.ecount[e]))
        op = Op("sp", None, "final")
        op.idx = len(self.seq)
        op.waits = waits
        self.seq.append(op)

    def emit_all(self):
        nc = self.nc
        streams = {e: [o for o in self.seq if o.eng == e] for e in ENGN}

        def run(stream, engobj):
            for o in stream:
                for (s, v) in o.waits:
                    engobj.wait_ge(s, v)
                if o.emit is None:
                    continue
                ins = o.emit(engobj)
                if o.inc is not None:
                    assert ins is not None, f"op {o.tag} returned no instruction"
                    ins.then_inc(o.inc[0], o.inc[1])

        with nc.Block() as block:
            if streams["pe"]:
                block.tensor(lambda e: run(streams["pe"], e))
            if streams["act"]:
                block.scalar(lambda e: run(streams["act"], e))
            if streams["dve"]:
                block.vector(lambda e: run(streams["dve"], e))
            if streams["pool"]:
                block.gpsimd(lambda e: run(streams["pool"], e))
            if streams["sp"]:
                block.sync(lambda e: run(streams["sp"], e))


WEIGHT_SHAPES = {
    "L0_attn_W_charge": [1, D], "L0_attn_b_charge": [1], "L0_attn_W_shell": [64, D],
    "L0_attn_b_shell": [64], "L0_attn_omega": [32, 64], "L0_attn_rff_bias": [32],
    "L0_attn_W_combine": [64, 320], "L0_attn_W_v": [D, D], "L0_attn_b_v": [D],
    "L0_attn_W_out": [D, D], "L0_attn_b_out": [D], "L0_attn_log_temp": [1],
    "L0_ffn_gate_w": [NE, D], "L0_ffn_expert_bias": [NE],
    "L0_ffn_w1": [NE, DFF, D], "L0_ffn_b1": [NE, DFF],
    "L0_ffn_w2": [NE, D, DFF], "L0_ffn_b2": [NE, D],
    "L0_ffn_w3": [NE, DFF, D], "L0_ffn_b3": [NE, DFF],
    "L0_ffn_sw1": [FDFF, D], "L0_ffn_sb1": [FDFF],
    "L0_ffn_sw2": [D, FDFF], "L0_ffn_sb2": [D],
    "L0_ffn_sw3": [FDFF, D], "L0_ffn_sb3": [FDFF],
}


def build_nc():
    nc = bass.Bass("TRN2", target_bir_lowering=False)
    S = Sched(nc)

    x_in = nc.dram_tensor("x_in", [N, D], F32, kind="ExternalInput").ap()
    out = nc.dram_tensor("out", [N, D], F32, kind="ExternalOutput").ap()
    W = {k: nc.dram_tensor(k, shp, F32, kind="ExternalInput").ap() for k, shp in WEIGHT_SHAPES.items()}

    wv_bf = nc.dram_tensor("wv_bf", [D, D], BF16, kind="Internal").ap()
    wo_bf = nc.dram_tensor("wo_bf", [D, D], BF16, kind="Internal").ap()
    w1_bf = nc.dram_tensor("w1_bf", [NE, DFF, D], BF16, kind="Internal").ap()
    w3_bf = nc.dram_tensor("w3_bf", [NE, DFF, D], BF16, kind="Internal").ap()
    w2_bf = nc.dram_tensor("w2_bf", [NE, D, DFF], BF16, kind="Internal").ap()
    sw1_bf = nc.dram_tensor("sw1_bf", [FDFF, D], BF16, kind="Internal").ap()
    sw3_bf = nc.dram_tensor("sw3_bf", [FDFF, D], BF16, kind="Internal").ap()
    sw2_bf = nc.dram_tensor("sw2_bf", [D, FDFF], BF16, kind="Internal").ap()
    rrow_dr = nc.dram_tensor("rrow_dr", [1, N], F32, kind="Internal").ap()

    cmv = np.zeros((128, 128), np.float32)
    cmv[np.triu_indices(128, 1)] = -1e30
    cmask_d = nc.inline_tensor(cmv.astype(ml_dtypes.bfloat16), "c_mask").ap()
    CONSTVALS = [1e-5, 1.0, math.pi / 2, -0.5, -1.0, 0.5, 0.0, 0.1,
                 12582912.0, -12582912.0, 0.0, 0.0]
    cv = np.tile(np.array(CONSTVALS, np.float32)[None, :], (128, 1))
    cvals_d = nc.inline_tensor(cv, "c_cvals").ap()
    pos = np.arange(N, dtype=np.float64)
    qexp_d = nc.inline_tensor(
        np.exp(-DECAY * pos)[None, :].astype(ml_dtypes.bfloat16).copy(), "c_qexp").ap()
    kexp_d = nc.inline_tensor(
        np.exp(DECAY * pos)[None, :].astype(ml_dtypes.bfloat16).copy(), "c_kexp").ap()

    A = nc.alloc_sbuf_tensor
    xres = A("xres", [128, NTT * D], F32)
    fmA = A("fmA", [128, NDC * N], BF16)
    fmB = A("fmB", [128, NTT * D], BF16)
    warena = A("warena", [128, 20480], BF16)
    hbuf = A("hbuf", [128, 8192], BF16)
    scr8 = A("scr8", [128, 2048], F32)
    pPT = A("pPT", [128, 2048], BF16)
    stage = A("stage", [128, 1024], BF16)
    s1a = A("s1a", [128, 512], F32)
    s1c = A("s1c", [128, 512], F32)
    ebias_bc = A("ebias_bc", [128, 16], F32)
    bshb = A("bshb", [128, 64], F32)
    bb1 = A("bb1", [128, 64], F32)
    bb3 = A("bb3", [128, 64], F32)
    bsb1 = A("bsb1", [128, 8], F32)
    bsb3 = A("bsb3", [128, 8], F32)
    idbf = A("idbf", [128, 128], BF16)
    idf = A("idf", [128, 128], F32)
    cmask = A("cmask", [128, 128], BF16)
    ones1 = A("ones1", [128, 128], BF16)
    cols = A("cols", [128, 16], F32)
    bno = A("bno", [128, 24], F32)
    aggo = A("aggo", [128, 4], F32)
    lncol = A("lncol", [128, 4], F32)
    shst = A("shst", [128, 64], BF16)
    shst2 = A("shst2", [128, 64], BF16)
    stage2 = A("stage2", [128, 1024], BF16)
    wsm = A("wsm", [128, 384], BF16)
    cvals = A("cvals", [128, 12], F32)
    sqs = s1a[:, 64:128]
    b2bf = hbuf[0:16, 0:1024]
    gatew = scr8[:, 512:768]
    gstat = scr8[:, 768:832]
    gatewT = scr8[:, 1024:2048].bitcast(BF16)
    shtm = scr8[:, 1536:1600]
    shtm2 = scr8[:, 1600:1664]

    PT_ = nc.alloc_psum_tensor
    psA = PT_("psA", [128, 512], F32)
    psB = PT_("psB", [128, 512], F32)
    psG = PT_("psG", [128, 512], F32)
    psH = PT_("psH", [128, 512], F32)
    psC = PT_("psC", [128, 512], F32)
    psD = PT_("psD", [128, 512], F32)
    psE = PT_("psE", [128, 1024], BF16)
    psF = PT_("psF", [128, 512], F32)

    dbg = {}
    if DEBUG:
        shapes_d = {"d_scrq": [128, 2048], "d_phif": [128, 2048], "d_srow": [128, 2048],
                    "d_gatew": [128, 256], "d_xattn": [128, 1024], "d_wsm": [128, 384],
                    "d_lnt": [128, 2048], "d_logits": [128, 256], "d_nats": [128, 1024]}
        for nm, shp in shapes_d.items():
            dbg[nm] = nc.dram_tensor(nm, shp, F32, kind="ExternalOutput").ap()

    def mm_group(ps_ap, pairs, tag):
        pairs = list(pairs)

        def emit(t, ps_ap=ps_ap, pairs=pairs):
            n = len(pairs)
            last = None
            for i, (l, r) in enumerate(pairs):
                last = t.matmul(ps_ap, l, r, start=(i == 0), stop=(i == n - 1))
            return last
        rd = [l for l, _ in pairs] + [r for _, r in pairs]
        return S.add("pe", emit, rd=rd, wr=[ps_ap], tag=tag)

    def mm_group2(psc, psd, lhsts, rhs_c, rhs_d, tag, pre_c=(), pre_d=(),
                  open_end=False, cont=False):
        psc = psc[:, :] if not hasattr(psc, "tensor") else psc
        psd = psd[:, :] if not hasattr(psd, "tensor") else psd
        lhsts = list(lhsts)
        rhs_c = list(rhs_c)
        rhs_d = list(rhs_d)
        pre_c = list(pre_c)
        pre_d = list(pre_d)

        def emit(t, psc=psc, psd=psd, lhsts=lhsts, rhs_c=rhs_c, rhs_d=rhs_d,
                 pre_c=pre_c, pre_d=pre_d, open_end=open_end, cont=cont):
            last = None
            for i, (l, r) in enumerate(pre_c):
                t.matmul(psc, l, r, start=(i == 0) and not cont, stop=False)
            for i, (l, r) in enumerate(pre_d):
                t.matmul(psd, l, r, start=(i == 0) and not cont, stop=False)
            n = len(lhsts)
            last = None
            for i, l in enumerate(lhsts):
                sc = (i == 0) and not pre_c and not cont
                sd = (i == 0) and not pre_d and not cont
                stp = (i == n - 1) and not open_end
                t.matmul(psc, l, rhs_c[i], start=sc, stop=stp)
                last = t.matmul(psd, l, rhs_d[i], start=sd, stop=stp)
            return last
        rd = lhsts + rhs_c + rhs_d + [a for p in pre_c for a in p] + [a for p in pre_d for a in p]
        return S.add("pe", emit, rd=rd, wr=[psc, psd], tag=tag)

    def tr_group(ps_slices, in_slices, iden, tag):
        ps_slices = list(ps_slices)
        in_slices = list(in_slices)
        idens = []
        for i in in_slices:
            b = i.base_partition()
            p = i.shape[0]
            idens.append(iden[b:b + p, b:b + p])

        def emit(t, ps_slices=ps_slices, in_slices=in_slices, idens=idens):
            last = None
            for o, i, idn in zip(ps_slices, in_slices, idens):
                last = t.transpose(o, i, idn)
            return last
        return S.add("pe", emit, rd=in_slices + [iden], wr=ps_slices, tag=tag)

    def dve(emit, rd, wr, tag=""):
        return S.add("dve", emit, rd=rd, wr=wr, tag=tag)

    def pool(emit, rd, wr, tag=""):
        return S.add("pool", emit, rd=rd, wr=wr, tag=tag)

    def act(emit, rd, wr, tag=""):
        return S.add("act", emit, rd=list(rd) + [cvals[:, :]], wr=wr, tag=tag)

    def dma(q, out_ap, in_ap, tag="", transpose=False):
        if transpose:
            return S.add_dma(q, lambda e, o=out_ap, i=in_ap: e.dma_start_transpose(o, i),
                             rd=[in_ap], wr=[out_ap], tag=tag)
        return S.add_dma(q, lambda e, o=out_ap, i=in_ap: e.dma_start(out=o, in_=i),
                         rd=[in_ap], wr=[out_ap], tag=tag)

    def ddump(name, src_ap, q="act"):
        if DEBUG and name in dbg:
            S.add_dma(q, lambda e, a=src_ap, o=dbg[name]: e.dma_start(out=o, in_=a),
                      rd=[src_ap], wr=[dbg[name]], tag=f"dbg_{name}")

    def xload(g, q="pool"):
        src = x_in[g * 512:(g + 1) * 512, :].rearrange("(t p) d -> p t d", p=128)
        dst = xres[:, g * 4096:(g + 1) * 4096].rearrange("p (t d) -> p t d", d=D)
        dma(q, dst, src, tag=f"xload{g}")
    dve(lambda e: e.memset(ones1[:, :], 1.0), rd=[], wr=[ones1[:, :]], tag="ones")
    dma("pool", pPT[0:1, 0:1024], W["L0_attn_W_charge"], tag="wch_nat")
    dma("pool", pPT[32:48, 0:1024], W["L0_ffn_gate_w"], tag="gw_nat")
    dma("pool", pPT[64:96, 0:64], W["L0_attn_omega"], tag="om_nat")
    dma("pool", pPT[0:64, 1024:1344], W["L0_attn_W_combine"], tag="wc_nat")
    pool(lambda e: e.affine_select(idbf[:, :], ones1[:, :], [[1, 128]], ALU.is_equal,
                                   0.0, channel_multiplier=-1),
         rd=[ones1[:, :]], wr=[idbf[:, :]], tag="idbf")
    dve(lambda e: e.tensor_copy(idf[:, :], idbf[:, :]), rd=[idbf[:, :]], wr=[idf[:, :]],
        tag="idf")
    xload(0)
    dma("pool", hbuf[0:64, 4096:5120], W["L0_attn_W_shell"], tag="wsh_stage")
    dma("pool", wv_bf, W["L0_attn_W_v"], tag="cast_wv")

    dma("sp", cvals[:, :], cvals_d, tag="cvals")
    for _i, _v in enumerate(CONSTVALS):
        nc.const_aps.register(F32, _v, cvals[:, _i:_i + 1])
    if KPHASE == "p_x":
        S.final_barrier()
        S.emit_all()
        return nc


    ddump("d_nats", pPT[:, 0:1024], q="pool")

    dma("act", scr8[32:40, 0:128], W["L0_ffn_sb1"].rearrange("(a b) -> a b", b=128), tag="sb1nat")
    dma("act", scr8[64:72, 0:128], W["L0_ffn_sb3"].rearrange("(a b) -> a b", b=128), tag="sb3nat")
    dma("act", scr8[0:16, 0:512], W["L0_ffn_b1"], tag="b1nat")
    dma("act", scr8[0:16, 512:1024], W["L0_ffn_b3"], tag="b3nat")
    dma("act", bshb[:, :], W["L0_attn_b_shell"][None, :].partition_broadcast(128), tag="bshb")
    dma("act", cols[0:1, 0:1], W["L0_attn_b_charge"][None, :], tag="bch")
    dma("act", cols[0:1, 4:5], W["L0_attn_log_temp"][None, :], tag="lt")
    dma("act", ebias_bc[:, :], W["L0_ffn_expert_bias"][None, :].partition_broadcast(128), tag="ebias")
    dma("act", cols[0:32, 2:3], W["L0_attn_rff_bias"].rearrange("(p o) -> p o", o=1), tag="rffcol")
    dma("act", cols[32:64, 2:3], W["L0_attn_rff_bias"].rearrange("(p o) -> p o", o=1), tag="rffcol2")
    for g in range(1, 4):
        xload(g, "act")
    dma("act", hbuf[0:64, 6144:8192], qexp_d[0:1, :].partition_broadcast(64), tag="qexp")
    if KPHASE == "p_rows":
        S.final_barrier()
        S.emit_all()
        return nc


    WVT, WOT, WSHT = 0, 8192, 16384

    tr_group([psE[:, dc * 128: dc * 128 + 1] for dc in range(8)],
             [pPT[0:1, dc * 128:(dc + 1) * 128] for dc in range(8)], idbf[:, :], "tr_wch")
    dve(lambda e: e.tensor_copy(wsm[:, 0:8].rearrange("p (a b) -> p a b", b=1),
                                psE[:, :].rearrange("p (a b) -> p a b", b=128)[:, :, 0:1]),
        rd=[psE[:, :]], wr=[wsm[:, 0:8]], tag="ev_wch")
    if KPHASE == "p_t1":
        S.final_barrier()
        S.emit_all()
        return nc

    tr_group([psE[0:64, 32:64]], [pPT[64:96, 0:64]], idbf[:, :], "tr_om")
    dve(lambda e: e.tensor_copy(wsm[0:64, 8:40], psE[0:64, 32:64]),
        rd=[psE[0:64, 32:64]], wr=[wsm[0:64, 8:40]], tag="ev_om")
    if KPHASE == "p_t2":
        S.final_barrier()
        S.emit_all()
        return nc

    tr_group([psE[:, 64 + dc * 16: 64 + (dc + 1) * 16] for dc in range(8)],
             [pPT[32:48, dc * 128:(dc + 1) * 128] for dc in range(8)], idbf[:, :], "tr_gw")
    dve(lambda e: e.tensor_copy(wsm[:, 232:360], psE[:, 64:192]),
        rd=[psE[:, 64:192]], wr=[wsm[:, 232:360]], tag="ev_gw")
    if KPHASE == "p_t3":
        S.final_barrier()
        S.emit_all()
        return nc

    tr_group([psF[:, 0:8]], [scr8[32:40, 0:128]], idf[:, :], "tr_sb1")
    dve(lambda e: e.tensor_copy(bsb1[:, :], psF[:, 0:8]), rd=[psF[:, 0:8]], wr=[bsb1[:, :]], tag="ev_sb1")
    tr_group([psF[:, 8:16]], [scr8[64:72, 0:128]], idf[:, :], "tr_sb3")
    dve(lambda e: e.tensor_copy(bsb3[:, :], psF[:, 8:16]), rd=[psF[:, 8:16]], wr=[bsb3[:, :]], tag="ev_sb3")
    for dstT, tg, c0 in [(bb1, "b1", 0), (bb3, "b3", 512)]:
        nat = scr8[0:16, c0:c0 + 512]
        tr_group([psF[:, 16 + fc * 16: 16 + (fc + 1) * 16] for fc in range(4)],
                 [nat[0:16, fc * 128:(fc + 1) * 128] for fc in range(4)], idf[:, :], f"tr_{tg}")
        dve(lambda e, d=dstT: e.tensor_copy(d[:, :], psF[:, 16:80]),
            rd=[psF[:, 16:80]], wr=[dstT[:, :]], tag=f"ev_{tg}")

    tr_group([psE[:, 192:256]], [pPT[0:64, 1024:1152]], idbf[:, :], "tr_wc0")
    dve(lambda e: e.tensor_copy(wsm[:, 40:104], psE[:, 192:256]),
        rd=[psE[:, 192:256]], wr=[wsm[:, 40:104]], tag="ev_wc0")
    dve(lambda e: e.tensor_scalar_mul(wsm[0:64, 40:104], wsm[0:64, 40:104],
                                      MAGS[0] * RFF_SCALE),
        rd=[wsm[0:64, 40:104]], wr=[wsm[0:64, 40:104]], tag="wc0_scale")
    tr_group([psE[:, 256:320]], [pPT[0:64, 1152:1280]], idbf[:, :], "tr_wc1")
    dve(lambda e: e.tensor_copy(wsm[:, 104:168], psE[:, 256:320]),
        rd=[psE[:, 256:320]], wr=[wsm[:, 104:168]], tag="ev_wc1")
    tr_group([psE[0:64, 320:384]], [pPT[0:64, 1280:1344]], idbf[:, :], "tr_wc2")
    dve(lambda e: e.tensor_copy(wsm[0:64, 168:232], psE[0:64, 320:384]),
        rd=[psE[0:64, 320:384]], wr=[wsm[0:64, 168:232]], tag="ev_wc2")
    if KPHASE == "p_tr":
        S.final_barrier()
        S.emit_all()
        return nc


    tr_group([psE[:, dc * 64:(dc + 1) * 64] for dc in range(NDC)],
             [hbuf[0:64, 4096 + dc * 128: 4096 + (dc + 1) * 128] for dc in range(NDC)],
             idbf[:, :], "tr_wsh")
    dve(lambda e: e.tensor_copy(warena[:, WSHT:WSHT + 512], psE[:, 0:512]),
        rd=[psE[:, 0:512]], wr=[warena[:, WSHT:WSHT + 512]], tag="ev_wsh")
    dma("pool", pPT[0:1, 0:1024], W["L0_attn_b_v"][None, :], tag="bv")
    ddump("d_wsm", wsm[:, :], q="pool")
    for dc in range(NDC):
        dma("sp", warena[:, WVT + dc * 1024: WVT + (dc + 1) * 1024],
            wv_bf[0:D, dc * 128:(dc + 1) * 128], tag=f"wvT{dc}", transpose=True)
    S.add_dma("pool", lambda e: e.dma_start(out=wo_bf, in_=W["L0_attn_W_out"]),
              rd=[W["L0_attn_W_out"], warena[:, WVT + 7 * 1024: WVT + 8 * 1024]],
              wr=[wo_bf], tag="cast_wo")
    for e_ in range(4):
        dma("pool", w1_bf[e_], W["L0_ffn_w1"][e_], tag=f"cw1_{e_}")
        dma("pool", w3_bf[e_], W["L0_ffn_w3"][e_], tag=f"cw3_{e_}")
        dma("pool", w2_bf[e_], W["L0_ffn_w2"][e_], tag=f"cw2_{e_}")
    for dc in range(NDC):
        dma("sp", warena[:, WOT + dc * 1024: WOT + (dc + 1) * 1024],
            wo_bf[0:D, dc * 128:(dc + 1) * 128], tag=f"woT{dc}", transpose=True)
    dma("sp", cmask[:, :], cmask_d, tag="cmask")
    if KPHASE == "p_b":
        S.final_barrier()
        S.emit_all()
        return nc


    if KPHASE == "prologue":
        S.final_barrier()
        S.emit_all()
        return nc
    def lnT(dc, c0, c1):
        return fmA[:, dc * N + c0: dc * N + c1]

    fmA_3d = fmA[:, :].rearrange("p (c t) -> p c t", t=N)

    def ln_stats(tt, tag):
        par = tt % 2
        bn = bno[:, par * 12:(par + 1) * 12]
        ag = aggo[:, par * 2:(par + 1) * 2]
        lc = lncol[:, par * 2:(par + 1) * 2]
        stg = stage if par == 0 else stage2
        xt = xres[:, tt * D:(tt + 1) * D]
        dve(lambda e, xt=xt, bn=bn: e.bn_stats(bn[:, 0:6], xt[:, 0:512]),
            rd=[xt[:, 0:512]], wr=[bn[:, 0:6]], tag=f"bna_{tag}")
        dve(lambda e, xt=xt, bn=bn: e.bn_stats(bn[:, 6:12], xt[:, 512:1024]),
            rd=[xt[:, 512:1024]], wr=[bn[:, 6:12]], tag=f"bnb_{tag}")
        dve(lambda e, bn=bn, ag=ag: e.bn_aggr(ag, bn), rd=[bn], wr=[ag], tag=f"bag_{tag}")
        act(lambda e, ag=ag, lc=lc: e.activation(lc[:, 0:1], ag[:, 1:2], AF.Ln, bias=1e-5),
            rd=[ag[:, 1:2]], wr=[lc[:, 0:1]], tag=f"lnv_{tag}")
        act(lambda e, lc=lc: e.activation(lc[:, 1:2], lc[:, 0:1], AF.Exp, scale=-0.5),
            rd=[lc[:, 0:1]], wr=[lc[:, 1:2]], tag=f"rstd_{tag}")
        dve(lambda e, xt=xt, ag=ag, lc=lc, stg=stg: e.tensor_scalar(
            stg[:, :], xt, ag[:, 0:1], lc[:, 1:2], ALU.subtract, ALU.mult),
            rd=[xt, ag[:, 0:1], lc[:, 1:2]], wr=[stg[:, :]], tag=f"lnm_{tag}")

    def ln_fin(tt, tag):
        par = tt % 2
        stg = stage if par == 0 else stage2
        tr_group([psE[:, dc * 128:(dc + 1) * 128] for dc in range(8)],
                 [stg[:, dc * 128:(dc + 1) * 128] for dc in range(8)], idbf[:, :], f"lntr_{tag}")
        dst = fmA_3d[:, :, tt * 128:(tt + 1) * 128]
        act(lambda e, dst=dst: e.copy(dst, psE[:, :].rearrange("p (c t) -> p c t", t=128)),
            rd=[psE[:, :]],
            wr=[fmA[:, dc * N + tt * 128: dc * N + (tt + 1) * 128] for dc in range(NDC)],
            tag=f"lnev_{tag}")

    def layernorm(tt, tag):
        ln_stats(tt, tag)
        ln_fin(tt, tag)

    brow0 = pPT[0:1, 0:1024]
    def vproj(tt, bk=None):
        pc, pd = bk if bk else ((psC, psD) if tt % 2 == 0 else (psF, psH))
        lhsts = [lnT(dc, tt * 128, (tt + 1) * 128) for dc in range(NDC)]
        rhs_c = [warena[:, WVT + dc * 1024: WVT + dc * 1024 + 512] for dc in range(NDC)]
        rhs_d = [warena[:, WVT + dc * 1024 + 512: WVT + (dc + 1) * 1024] for dc in range(NDC)]
        mm_group2(pc, pd, lhsts, rhs_c, rhs_d, f"vp_{tt}",
                  pre_c=[(ones1[0:1, :], brow0[0:1, 0:512])],
                  pre_d=[(ones1[0:1, :], brow0[0:1, 512:1024])])
        act(lambda e, tt=tt, pc=pc: e.copy(fmB[:, tt * D: tt * D + 512], pc[:, :]),
            rd=[pc[:, :]], wr=[fmB[:, tt * D: tt * D + 512]], tag=f"vev0_{tt}")
        act(lambda e, tt=tt, pd=pd: e.copy(fmB[:, tt * D + 512: (tt + 1) * D], pd[:, :]),
            rd=[pd[:, :]], wr=[fmB[:, tt * D + 512: (tt + 1) * D]], tag=f"vev1_{tt}")

    shptr = [0]

    def shell_mm(tt):
        par = tt % 2
        ps = psA if par == 0 else psB
        shm = shtm[:, par * 64:(par + 1) * 64] if False else shtm
        shs = shst[:, par * 64:(par + 1) * 64] if False else shst
        pairs = [(lnT(dc, tt * 128, (tt + 1) * 128), warena[:, WSHT + dc * 64: WSHT + (dc + 1) * 64])
                 for dc in range(NDC)]
        mm_group(ps[:, 0:64], pairs, f"sh_{tt}")
        shm = shtm2 if par else shtm
        shs = shst2 if par else shst
        c9 = cols[:, 9:10] if par == 0 else lncol[:, 2:3]
        c10 = cols[:, 10:11] if par == 0 else lncol[:, 3:4]
        dve(lambda e, ps=ps, shm=shm: e.tensor_tensor(shm[:, :], ps[:, 0:64], bshb[:, :], ALU.add),
            rd=[ps[:, 0:64], bshb[:, :]], wr=[shm[:, :]], tag=f"shb_{tt}")
        act(lambda e, shm=shm, c9=c9: e.activation(sqs[:, :], shm[:, :], AF.Square, accum_out=c9),
            rd=[shm[:, :]], wr=[sqs[:, :], c9], tag=f"shsq_{tt}")
        act(lambda e, c9=c9, c10=c10: e.activation(c10, c9, AF.Ln),
            rd=[c9], wr=[c10], tag=f"shln_{tt}")
        act(lambda e, c10=c10: e.activation(c10, c10, AF.Exp, scale=0.5),
            rd=[c10], wr=[c10], tag=f"shex_{tt}")
        dve(lambda e, c10=c10: e.tensor_scalar_max(c10, c10, 1e-12),
            rd=[c10], wr=[c10], tag=f"shmx_{tt}")
        dve(lambda e, c9=c9, c10=c10: e.reciprocal(c9, c10),
            rd=[c10], wr=[c9], tag=f"shrc_{tt}")
        dve(lambda e, shm=shm, c9=c9: e.tensor_scalar_mul(shm[:, :], shm[:, :], c9),
            rd=[shm[:, :], c9], wr=[shm[:, :]], tag=f"shmul_{tt}")

    def shell_fin(tt):
        par = tt % 2
        shm = shtm2 if par else shtm
        tr_group([psG[0:64, 0:128]], [shm[:, :]], idf[:, :], f"shtr_{tt}")
        dve(lambda e, tt=tt: e.tensor_copy(hbuf[0:64, tt * 128:(tt + 1) * 128], psG[0:64, 0:128]),
            rd=[psG[0:64, 0:128]], wr=[hbuf[0:64, tt * 128:(tt + 1) * 128]], tag=f"shev_{tt}")

    def q_chunk(tc):
        qb = psF if tc % 2 == 0 else psC
        pairs = [(wsm[:, dc:dc + 1], lnT(dc, tc * 512, (tc + 1) * 512)) for dc in range(NDC)]
        mm_group(qb[0:1, 0:512], pairs, f"q_{tc}")
        act(lambda e, tc=tc, qb=qb: e.activation(scr8[0:1, tc * 512:(tc + 1) * 512], qb[0:1, 0:512],
                                                 AF.Tanh, bias=cols[0:1, 0:1]),
            rd=[qb[0:1, 0:512], cols[0:1, 0:1]], wr=[scr8[0:1, tc * 512:(tc + 1) * 512]], tag=f"qt_{tc}")
        dma("act", rrow_dr[0:1, tc * 512:(tc + 1) * 512], scr8[0:1, tc * 512:(tc + 1) * 512],
            tag=f"qst_{tc}")

    for tt in range(NTT + 3):
        if 1 <= tt <= NTT:
            ln_fin(tt - 1, f"l1_{tt - 1}")
        if tt < NTT:
            ln_stats(tt, f"l1_{tt}")
        if 2 <= tt <= NTT + 1:
            if 8 <= tt:
                vproj(tt - 8)
            shell_mm(tt - 2)
        if 3 <= tt <= NTT + 2:
            shell_fin(tt - 3)
        if 14 <= tt <= 16:
            q_chunk(tt - 14)
        if tt == 18:
            q_chunk(3)

    if KPHASE == "ln1":
        S.final_barrier()
        S.emit_all()
        return nc
    ddump("d_lnt", fmA[:, 0:2048], q="pool")

    if KPHASE == "vproj":
        S.final_barrier()
        S.emit_all()
        return nc
    QE, KE, PHIQ, PHIK = 6144, 2048, 0, 4096
    BIGC = 12582912.0
    PBANK = [psF, psH]
    dve(lambda e: e.tensor_scalar_add(cols[0:32, 2:3], cols[0:32, 2:3], math.pi / 2),
        rd=[cols[0:32, 2:3]], wr=[cols[0:32, 2:3]], tag="rffpi2")

    def proj_mm(tc):
        pairs = [(wsm[0:64, 8:40], hbuf[0:64, tc * 512:(tc + 1) * 512])]
        mm_group(psG[0:32, 0:512], pairs, f"proj_{tc}")
        mm_group(psG[32:64, 0:512], pairs, f"projb_{tc}")
        pb = PBANK[tc % 2][0:64, 0:512]
        dve(lambda e, pb=pb: e.tensor_scalar(pb, psG[0:64, 0:512], cols[0:64, 2:3],
                                             64.0 * TWO_PI, ALU.add, ALU.add),
            rd=[psG[0:64, 0:512], cols[0:64, 2:3]], wr=[pb], tag=f"prb_{tc}")

    def redsin64(tc):
        src = PBANK[tc % 2][0:64, 0:512]
        kf = scr8[0:64, (tc % 2) * 512:(tc % 2) * 512 + 512]
        tbuf = (s1a if tc % 2 == 0 else s1c)[0:64, 0:512]
        dst = hbuf[0:64, tc * 512:(tc + 1) * 512]
        act(lambda e, kf=kf, src=src: e.activation(kf, src, AF.Copy, bias=BIGC,
                                                   scale=1.0 / TWO_PI),
            rd=[src], wr=[kf], tag=f"k_{tc}")
        act(lambda e, kf=kf: e.activation(kf, kf, AF.Copy, bias=-BIGC),
            rd=[kf], wr=[kf], tag=f"ks_{tc}")
        dve(lambda e, kf=kf, src=src, tbuf=tbuf: e.scalar_tensor_tensor(
            tbuf, kf, -C1, src, ALU.mult, ALU.add),
            rd=[kf, src], wr=[tbuf], tag=f"t1_{tc}")
        dve(lambda e, kf=kf, tbuf=tbuf: e.scalar_tensor_tensor(
            tbuf, kf, -C2, tbuf, ALU.mult, ALU.add),
            rd=[kf, tbuf], wr=[tbuf], tag=f"t2_{tc}")
        dve(lambda e, tbuf=tbuf: e.tensor_scalar(tbuf, tbuf, PI_LO, -PI_LO, ALU.min, ALU.max),
            rd=[tbuf], wr=[tbuf], tag=f"tcl_{tc}")
        act(lambda e, dst=dst, tbuf=tbuf: e.activation(dst, tbuf, AF.Sin),
            rd=[tbuf], wr=[dst], tag=f"sin_{tc}")

    def tp_half(hh):
        c0, c1 = hh * 1024, (hh + 1) * 1024
        qbA = scr8[0:64, 0:1024]
        qbB = scr8[0:64, 1024:2048]
        ph = hbuf[0:64, c0:c1]
        dma("act", qbA, rrow_dr[0:1, c0:c1].partition_broadcast(64), tag=f"qbc{hh}")
        act(lambda e, qbA=qbA, qbB=qbB: e.activation(qbB, qbA, AF.Square),
            rd=[qbA], wr=[qbB], tag=f"pw2_{hh}")
        dve(lambda e, qbA=qbA, ph=ph, c0=c0: e.scalar_tensor_tensor(
            hbuf[64:128, c0:c0 + 1024], qbA, MAGS[1] * RFF_SCALE, ph, ALU.mult, ALU.mult),
            rd=[qbA, ph], wr=[hbuf[64:128, c0:c0 + 1024]], tag=f"tp1_{hh}")
        dve(lambda e, qbB=qbB, ph=ph, c0=c0: e.scalar_tensor_tensor(
            hbuf[0:64, 2048 + c0: 2048 + c0 + 1024], qbB, MAGS[2] * RFF_SCALE, ph,
            ALU.mult, ALU.mult),
            rd=[qbB, ph], wr=[hbuf[0:64, 2048 + c0: 2048 + c0 + 1024]], tag=f"tp2_{hh}")
        dve(lambda e, qbA=qbA, qbB=qbB: e.tensor_tensor(qbA, qbA, qbB, ALU.mult),
            rd=[qbA, qbB], wr=[qbA], tag=f"pw3_{hh}")
        dve(lambda e, qbA=qbA, ph=ph, c0=c0: e.scalar_tensor_tensor(
            hbuf[64:128, 2048 + c0: 2048 + c0 + 1024], qbA, MAGS[3] * RFF_SCALE, ph,
            ALU.mult, ALU.mult),
            rd=[qbA, ph], wr=[hbuf[64:128, 2048 + c0: 2048 + c0 + 1024]], tag=f"tp3_{hh}")
        act(lambda e, qbB=qbB: e.activation(qbB, qbB, AF.Square),
            rd=[qbB], wr=[qbB], tag=f"pw4_{hh}")
        dve(lambda e, qbB=qbB, ph=ph, c0=c0: e.scalar_tensor_tensor(
            hbuf[0:64, 4096 + c0: 4096 + c0 + 1024], qbB, MAGS[4] * RFF_SCALE, ph,
            ALU.mult, ALU.mult),
            rd=[qbB, ph], wr=[hbuf[0:64, 4096 + c0: 4096 + c0 + 1024]], tag=f"tp4_{hh}")

    def phif(tc):
        ps = psA if tc % 2 == 0 else psB
        pairs = [(wsm[:, 40:104], hbuf[:, tc * 512:(tc + 1) * 512]),
                 (wsm[:, 104:168], hbuf[:, 2048 + tc * 512: 2048 + (tc + 1) * 512]),
                 (wsm[0:64, 168:232], hbuf[0:64, 4096 + tc * 512: 4096 + (tc + 1) * 512])]
        mm_group(ps[0:64, 0:512], pairs, f"phi_{tc}")
        dma("act", hbuf[0:64, KE + tc * 512: KE + (tc + 1) * 512],
            kexp_d[0:1, tc * 512:(tc + 1) * 512].partition_broadcast(64), tag=f"kexp_{tc}")
        dve(lambda e, ps=ps, tc=tc: e.tensor_tensor(
            hbuf[0:64, PHIQ + tc * 512: PHIQ + (tc + 1) * 512], ps[0:64, 0:512],
            hbuf[0:64, QE + tc * 512: QE + (tc + 1) * 512], ALU.mult),
            rd=[ps[0:64, 0:512], hbuf[0:64, QE + tc * 512: QE + (tc + 1) * 512]],
            wr=[hbuf[0:64, PHIQ + tc * 512: PHIQ + (tc + 1) * 512]], tag=f"phiq_{tc}")
        dve(lambda e, ps=ps, tc=tc: e.tensor_tensor(
            hbuf[0:64, PHIK + tc * 512: PHIK + (tc + 1) * 512], ps[0:64, 0:512],
            hbuf[0:64, KE + tc * 512: KE + (tc + 1) * 512], ALU.mult),
            rd=[ps[0:64, 0:512], hbuf[0:64, KE + tc * 512: KE + (tc + 1) * 512]],
            wr=[hbuf[0:64, PHIK + tc * 512: PHIK + (tc + 1) * 512]], tag=f"phik_{tc}")

    proj_mm(0)
    redsin64(0)
    vproj(10, (psC, psD))
    proj_mm(1)
    redsin64(1)
    vproj(11, (psF, psH))
    tp_half(0)
    vproj(12, (psC, psD))
    phif(0)
    phif(1)
    proj_mm(2)
    redsin64(2)
    vproj(13, (psF, psH))
    proj_mm(3)
    redsin64(3)
    vproj(14, (psC, psD))
    vproj(15, (psF, psH))
    tp_half(1)
    phif(2)
    phif(3)
    ddump("d_phif", hbuf[:, 0:2048], q="pool")

    act(lambda e: e.activation(cols[0:1, 5:6], cols[0:1, 4:5], AF.Relu),
        rd=[cols[0:1, 4:5]], wr=[cols[0:1, 5:6]], tag="sp_relu")
    act(lambda e: e.activation(cols[0:1, 6:7], cols[0:1, 4:5], AF.Abs),
        rd=[cols[0:1, 4:5]], wr=[cols[0:1, 6:7]], tag="sp_abs")
    act(lambda e: e.activation(cols[0:1, 7:8], cols[0:1, 6:7], AF.Exp, scale=-1.0),
        rd=[cols[0:1, 6:7]], wr=[cols[0:1, 7:8]], tag="sp_exp")
    act(lambda e: e.activation(cols[0:1, 6:7], cols[0:1, 7:8], AF.Ln, bias=1.0),
        rd=[cols[0:1, 7:8]], wr=[cols[0:1, 6:7]], tag="sp_ln")
    dve(lambda e: e.tensor_tensor(cols[0:1, 5:6], cols[0:1, 5:6], cols[0:1, 6:7], ALU.add),
        rd=[cols[0:1, 5:6], cols[0:1, 6:7]], wr=[cols[0:1, 5:6]], tag="sp_add")
    dve(lambda e: e.tensor_scalar_add(cols[0:1, 5:6], cols[0:1, 5:6], 0.1),
        rd=[cols[0:1, 5:6]], wr=[cols[0:1, 5:6]], tag="sp_add2")
    dve(lambda e: e.reciprocal(cols[0:1, 8:9], cols[0:1, 5:6]),
        rd=[cols[0:1, 5:6]], wr=[cols[0:1, 8:9]], tag="sp_recip")
    onef = s1c[0:1, 0:128]
    dve(lambda e: e.memset(onef, 1.0), rd=[], wr=[onef], tag="onesf")
    mm_group(psF[:, 80:81], [(onef, cols[0:1, 8:9])], "bc_invt")
    dve(lambda e: e.tensor_copy(cols[:, 3:4], psF[:, 80:81]),
        rd=[psF[:, 80:81]], wr=[cols[:, 3:4]], tag="ev_invt")

    if KPHASE == "phi":
        S.final_barrier()
        S.emit_all()
        return nc
    def gate_mm(tt):
        ps_g = psF
        pairs = [(lnT(dc, tt * 128, (tt + 1) * 128), wsm[:, 232 + dc * 16: 232 + (dc + 1) * 16])
                 for dc in range(NDC)]
        mm_group(ps_g[:, 100:116], pairs, f"g_{tt}")
        dve(lambda e, tt=tt, ps_g=ps_g: e.tensor_copy(s1a[:, tt * 16:(tt + 1) * 16], ps_g[:, 100:116]),
            rd=[ps_g[:, 100:116]], wr=[s1a[:, tt * 16:(tt + 1) * 16]], tag=f"gev_{tt}")

    brow_o = hbuf[0:1, 3072:4096]
    dma("pool", brow_o, W["L0_attn_b_out"][None, :], tag="bo")

    def out_proj(tt):
        lhsts = [lnT(dc, tt * 128, (tt + 1) * 128) for dc in range(NDC)]
        rhs_c = [warena[:, WOT + dc * 1024: WOT + dc * 1024 + 512] for dc in range(NDC)]
        rhs_d = [warena[:, WOT + dc * 1024 + 512: WOT + (dc + 1) * 1024] for dc in range(NDC)]
        mm_group2(psC, psD, lhsts, rhs_c, rhs_d, f"op_{tt}",
                  pre_c=[(ones1[0:1, :], brow_o[0:1, 0:512])],
                  pre_d=[(ones1[0:1, :], brow_o[0:1, 512:1024])])
        xs0 = xres[:, tt * D: tt * D + 512]
        xs1 = xres[:, tt * D + 512: (tt + 1) * D]
        dve(lambda e, xs0=xs0: e.tensor_tensor(xs0, psC[:, :], xs0, ALU.add),
            rd=[psC[:, :], xs0], wr=[xs0], tag=f"orev0_{tt}")
        dve(lambda e, xs1=xs1: e.tensor_tensor(xs1, psD[:, :], xs1, ALU.add),
            rd=[psD[:, :], xs1], wr=[xs1], tag=f"orev1_{tt}")

    def s_mms(qi):
        lq = hbuf[0:64, PHIQ + qi * 128: PHIQ + (qi + 1) * 128]
        nf = qi // 4
        SROT = (psA, psB, psH)
        for b in range(nf):
            ps = SROT[b % 3]
            mm_group(ps[:, 0:512], [(lq, hbuf[0:64, PHIK + b * 512: PHIK + (b + 1) * 512])], f"s{qi}_{b}")
            dve(lambda e, ps=ps, b=b, qi=qi: e.tensor_copy(scr8[:, b * 512:(b + 1) * 512], ps[:, 0:512]),
                rd=[ps[:, 0:512]], wr=[scr8[:, b * 512:(b + 1) * 512]], tag=f"sev{qi}_{b}")
        rem = qi - 4 * nf
        if rem:
            ps = SROT[nf % 3]
            mm_group(ps[:, 0:rem * 128],
                     [(lq, hbuf[0:64, PHIK + 4 * nf * 128: PHIK + qi * 128])], f"sr{qi}")
            dve(lambda e, ps=ps, nf=nf, rem=rem, qi=qi: e.tensor_copy(
                scr8[:, 4 * nf * 128: qi * 128], ps[:, 0:rem * 128]),
                rd=[ps[:, 0:rem * 128]], wr=[scr8[:, 4 * nf * 128: qi * 128]], tag=f"srev{qi}")
        mm_group(psG[:, 0:128], [(lq, hbuf[0:64, PHIK + qi * 128: PHIK + (qi + 1) * 128])], f"sd{qi}")
        dve(lambda e, qi=qi: e.tensor_tensor(scr8[:, qi * 128:(qi + 1) * 128],
                                             psG[:, 0:128], cmask[:, :], ALU.add),
            rd=[psG[:, 0:128], cmask[:, :]], wr=[scr8[:, qi * 128:(qi + 1) * 128]], tag=f"sdev{qi}")

    s_mms(0)
    for qi in range(NTT):
        Wd = (qi + 1) * 128
        nf = qi // 4
        for b in range(nf):
            dve(lambda e, b=b: e.reduce_max(cols[:, 5 + b: 6 + b], scr8[:, b * 512:(b + 1) * 512], AX.X),
                rd=[scr8[:, b * 512:(b + 1) * 512]], wr=[cols[:, 5 + b: 6 + b]], tag=f"spm{qi}_{b}")
        dve(lambda e, nf=nf, Wd=Wd: e.reduce_max(cols[:, 5 + nf: 6 + nf],
                                                 scr8[:, 4 * nf * 128: Wd], AX.X),
            rd=[scr8[:, 4 * nf * 128: Wd]], wr=[cols[:, 5 + nf: 6 + nf]], tag=f"stm{qi}")
        dve(lambda e, nf=nf: e.reduce_max(cols[:, 9:10], cols[:, 5: 6 + nf], AX.X),
            rd=[cols[:, 5: 6 + nf]], wr=[cols[:, 9:10]], tag=f"smx{qi}")
        dve(lambda e: e.tensor_scalar(cols[:, 10:11], cols[:, 9:10], cols[:, 3:4], -1.0,
                                      ALU.mult, ALU.mult),
            rd=[cols[:, 9:10], cols[:, 3:4]], wr=[cols[:, 10:11]], tag=f"snb{qi}")
        if qi > 0:
            out_proj(qi - 1)
        if qi == 3:
            ddump("d_srow", scr8[:, :])
        nck = (Wd + 511) // 512
        for b in range(nck):
            c0, c1 = b * 512, min((b + 1) * 512, Wd)
            act(lambda e, c0=c0, c1=c1, b=b: e.activation(pPT[:, c0:c1], scr8[:, c0:c1], AF.Exp,
                                                          bias=cols[:, 10:11], scale=cols[:, 3:4],
                                                          accum_out=cols[:, 5 + b: 6 + b]),
                rd=[scr8[:, c0:c1], cols[:, 10:11], cols[:, 3:4]],
                wr=[pPT[:, c0:c1], cols[:, 5 + b: 6 + b]], tag=f"sexp{qi}_{b}")
            npt = (c1 - c0 + 127) // 128
            tr_group([psE[:, i * 128:(i + 1) * 128] for i in range(npt)],
                     [pPT[:, c0 + i * 128: c0 + (i + 1) * 128] for i in range(npt)],
                     idbf[:, :], f"ptr{qi}_{b}")
            dve(lambda e, c0=c0, c1=c1: e.tensor_copy(pPT[:, c0:c1], psE[:, 0:c1 - c0]),
                rd=[psE[:, 0:c1 - c0]], wr=[pPT[:, c0:c1]], tag=f"ptev{qi}_{b}")
        if qi + 1 < NTT:
            s_mms(qi + 1)
        dve(lambda e, nck=nck: e.reduce_sum(cols[:, 11:12], cols[:, 5:5 + nck], AX.X),
            rd=[cols[:, 5:5 + nck]], wr=[cols[:, 11:12]], tag=f"ssum{qi}")
        dve(lambda e: e.reciprocal(cols[:, 12:13], cols[:, 11:12]),
            rd=[cols[:, 11:12]], wr=[cols[:, 12:13]], tag=f"srs{qi}")
        for b0 in range(0, qi + 1, 4):
            b1 = min(b0 + 4, qi + 1)
            lhsts = [pPT[:, kc * 128:(kc + 1) * 128] for kc in range(b0, b1)]
            rhs_c = [fmB[:, kc * D: kc * D + 512] for kc in range(b0, b1)]
            rhs_d = [fmB[:, kc * D + 512: (kc + 1) * D] for kc in range(b0, b1)]
            mm_group2(psC, psD, lhsts, rhs_c, rhs_d, f"pv{qi}_{b0}",
                      cont=(b0 > 0), open_end=(b1 < qi + 1))
        dve(lambda e: e.tensor_scalar_mul(hbuf[:, 2048:2560], psC[:, :], cols[:, 12:13]),
            rd=[psC[:, :], cols[:, 12:13]], wr=[hbuf[:, 2048:2560]], tag=f"oev0{qi}")
        dve(lambda e: e.tensor_scalar_mul(hbuf[:, 2560:3072], psD[:, :], cols[:, 12:13]),
            rd=[psD[:, :], cols[:, 12:13]], wr=[hbuf[:, 2560:3072]], tag=f"oev1{qi}")
        tr_group([psE[:, dc * 128:(dc + 1) * 128] for dc in range(8)],
                 [hbuf[:, 2048 + dc * 128: 2048 + (dc + 1) * 128] for dc in range(8)],
                 idbf[:, :], f"otr{qi}")
        dst = fmA_3d[:, :, qi * 128:(qi + 1) * 128]
        dve(lambda e, dst=dst: e.tensor_copy(dst, psE[:, :].rearrange("p (c t) -> p c t", t=128)),
            rd=[psE[:, :]],
            wr=[fmA[:, dc * N + qi * 128: dc * N + (qi + 1) * 128] for dc in range(NDC)],
            tag=f"otev{qi}")
        if qi >= 2:
            ln_fin(qi - 2, f"l2_{qi - 2}")
        if qi >= 3:
            gate_mm(qi - 3)
        if qi > 0:
            ln_stats(qi - 1, f"l2_{qi - 1}")

    if KPHASE == "sloop":
        S.final_barrier()
        S.emit_all()
        return nc
    out_proj(NTT - 1)
    ln_stats(NTT - 1, f"l2_{NTT - 1}")
    ln_fin(NTT - 2, f"l2_{NTT - 2}")
    gate_mm(NTT - 3)
    ln_fin(NTT - 1, f"l2_{NTT - 1}")
    gate_mm(NTT - 2)
    gate_mm(NTT - 1)
    ddump("d_xattn", xres[:, 0:1024])

    if KPHASE == "oproj":
        S.final_barrier()
        S.emit_all()
        return nc

    if KPHASE == "ln2":
        S.final_barrier()
        S.emit_all()
        return nc
    for e_ in range(4, NE):
        dma("pool", w1_bf[e_], W["L0_ffn_w1"][e_], tag=f"cw1_{e_}")
        dma("pool", w3_bf[e_], W["L0_ffn_w3"][e_], tag=f"cw3_{e_}")
        dma("pool", w2_bf[e_], W["L0_ffn_w2"][e_], tag=f"cw2_{e_}")
    dma("pool", sw1_bf, W["L0_ffn_sw1"], tag="c_sw1")
    dma("pool", sw3_bf, W["L0_ffn_sw3"], tag="c_sw3")
    dma("pool", sw2_bf, W["L0_ffn_sw2"], tag="c_sw2")

    W2S = 16384

    def ln2T(dc, c0, c1):
        return fmA[:, dc * N + c0: dc * N + c1]

    def load_expert_w13(e_):
        slot = e_ % 2
        W1S = slot * 8192
        W3S = slot * 8192 + 4096
        for dc in range(NDC):
            dma("sp", warena[:, W1S + dc * 512: W1S + (dc + 1) * 512],
                w1_bf[e_][0:DFF, dc * 128:(dc + 1) * 128], tag=f"w1T{e_}_{dc}", transpose=True)
        for dc in range(NDC):
            dma("sp", warena[:, W3S + dc * 512: W3S + (dc + 1) * 512],
                w3_bf[e_][0:DFF, dc * 128:(dc + 1) * 128], tag=f"w3T{e_}_{dc}", transpose=True)

    def load_expert_w2(e_):
        for fc in range(4):
            dma("sp", warena[:, W2S + fc * 1024: W2S + (fc + 1) * 1024],
                w2_bf[e_][0:D, fc * 128:(fc + 1) * 128], tag=f"w2T{e_}_{fc}", transpose=True)

    def h_phase_tc(e_, tc):
        slot = e_ % 2
        W1S = slot * 8192
        W3S = slot * 8192 + 4096
        if True:
            for fc in range(4):
                k = (tc * 4 + fc) % 2
                p1 = psA if k == 0 else psG
                p3 = psB if k == 0 else psH
                s1x = s1a if k == 0 else s1c
                mm_group(p1[:, :], [(warena[:, W1S + dc * 512 + fc * 128: W1S + dc * 512 + (fc + 1) * 128],
                                     ln2T(dc, tc * 512, (tc + 1) * 512)) for dc in range(NDC)],
                         f"h1_{e_}_{tc}_{fc}")
                mm_group(p3[:, :], [(warena[:, W3S + dc * 512 + fc * 128: W3S + dc * 512 + (fc + 1) * 128],
                                     ln2T(dc, tc * 512, (tc + 1) * 512)) for dc in range(NDC)],
                         f"h3_{e_}_{tc}_{fc}")
                bc1 = bb1[:, fc * 16 + e_: fc * 16 + e_ + 1]
                bc3 = bb3[:, fc * 16 + e_: fc * 16 + e_ + 1]
                act(lambda e, p1=p1, s1x=s1x, bc1=bc1: e.activation(s1x[:, :], p1[:, :], AF.Silu, bias=bc1),
                    rd=[p1[:, :], bc1], wr=[s1x[:, :]], tag=f"si_{e_}_{tc}_{fc}")
                hd = hbuf[:, fc * 2048 + tc * 512: fc * 2048 + (tc + 1) * 512]
                dve(lambda e, p3=p3, s1x=s1x, bc3=bc3, hd=hd: e.scalar_tensor_tensor(
                    hd, p3[:, :], bc3, s1x[:, :], ALU.add, ALU.mult),
                    rd=[p3[:, :], bc3, s1x[:, :]], wr=[hd], tag=f"hm_{e_}_{tc}_{fc}")
    def w2_phase(e_):
        for ts_ in range(NTT):
            pc_ = psC if ts_ % 2 == 0 else psF
            pd_ = psD if ts_ % 2 == 0 else psH
            lhsts = [hbuf[:, fc * 2048 + ts_ * 128: fc * 2048 + (ts_ + 1) * 128] for fc in range(4)]
            rhs_c = [warena[:, W2S + fc * 1024: W2S + fc * 1024 + 512] for fc in range(4)]
            rhs_d = [warena[:, W2S + fc * 1024 + 512: W2S + (fc + 1) * 1024] for fc in range(4)]
            mm_group2(pc_, pd_, lhsts, rhs_c, rhs_d, f"y_{e_}_{ts_}")
            gcol = gatew[:, ts_ * 16 + e_: ts_ * 16 + e_ + 1]
            xs0 = xres[:, ts_ * D: ts_ * D + 512]
            xs1 = xres[:, ts_ * D + 512: (ts_ + 1) * D]
            dve(lambda e, xs0=xs0, gcol=gcol, pc_=pc_: e.scalar_tensor_tensor(xs0, pc_[:, :], gcol, xs0,
                                                                              ALU.mult, ALU.add),
                rd=[pc_[:, :], gcol, xs0], wr=[xs0], tag=f"ye0_{e_}_{ts_}")
            dve(lambda e, xs1=xs1, gcol=gcol, pd_=pd_: e.scalar_tensor_tensor(xs1, pd_[:, :], gcol, xs1,
                                                                              ALU.mult, ALU.add),
                rd=[pd_[:, :], gcol, xs1], wr=[xs1], tag=f"ye1_{e_}_{ts_}")

    if KPHASE == "experts":
        S.final_barrier()
        S.emit_all()
        return nc
    load_expert_w13(0)
    load_expert_w2(0)


    logits = s1a[:, 0:256]
    ddump("d_logits", s1a[:, 0:256])
    act(lambda e: e.activation(s1c[:, 0:256], logits, AF.Sigmoid),
        rd=[logits], wr=[s1c[:, 0:256]], tag="gsig")
    v3 = lambda ap: ap.rearrange("p (a b) -> p a b", b=16)
    eb_b = ebias_bc[:, 0:16].unsqueeze(1).broadcast_to([128, 16, 16])
    dve(lambda e: e.tensor_tensor(v3(s1a[:, 256:512]), v3(logits), eb_b, ALU.add),
        rd=[logits, ebias_bc[:, 0:16]], wr=[s1a[:, 256:512]], tag="gbias")
    biased = s1a[:, 256:512]
    dve(lambda e: e.reduce_max(gstat[:, 0:16], v3(biased), AX.X),
        rd=[biased], wr=[gstat[:, 0:16]], tag="gm1")
    m1b = gstat[:, 0:16].unsqueeze(2).broadcast_to([128, 16, 16])
    dve(lambda e: e.tensor_tensor(v3(s1c[:, 256:512]), v3(biased), m1b, ALU.is_equal),
        rd=[biased, gstat[:, 0:16]], wr=[s1c[:, 256:512]], tag="gmk1")
    mask1 = s1c[:, 256:512]
    dve(lambda e: e.scalar_tensor_tensor(logits, mask1, -1e30, biased, ALU.mult, ALU.add),
        rd=[mask1, biased], wr=[logits], tag="gb2")
    b2t = logits
    dve(lambda e: e.reduce_max(gstat[:, 16:32], v3(b2t), AX.X),
        rd=[b2t], wr=[gstat[:, 16:32]], tag="gm2")
    m2b = gstat[:, 16:32].unsqueeze(2).broadcast_to([128, 16, 16])
    dve(lambda e: e.tensor_tensor(v3(biased), v3(b2t), m2b, ALU.is_equal),
        rd=[b2t, gstat[:, 16:32]], wr=[biased], tag="gmk2")
    mask2 = biased
    dve(lambda e: e.tensor_tensor(mask1, mask1, mask2, ALU.add),
        rd=[mask1, mask2], wr=[mask1], tag="gmsk")
    dve(lambda e: e.tensor_tensor(b2t, mask1, s1c[:, 0:256], ALU.mult),
        rd=[mask1, s1c[:, 0:256]], wr=[b2t], tag="gwsel")
    wsel = b2t
    dve(lambda e: e.reduce_sum(gstat[:, 32:48], v3(wsel), AX.X),
        rd=[wsel], wr=[gstat[:, 32:48]], tag="gden")
    dve(lambda e: e.tensor_scalar_add(gstat[:, 32:48], gstat[:, 32:48], 1e-10),
        rd=[gstat[:, 32:48]], wr=[gstat[:, 32:48]], tag="gden2")
    dve(lambda e: e.reciprocal(gstat[:, 48:64], gstat[:, 32:48]),
        rd=[gstat[:, 32:48]], wr=[gstat[:, 48:64]], tag="grd")
    rdb = gstat[:, 48:64].unsqueeze(2).broadcast_to([128, 16, 16])
    dve(lambda e: e.tensor_tensor(v3(gatew[:, 0:256]), v3(wsel), rdb, ALU.mult),
        rd=[wsel, gstat[:, 48:64]], wr=[gatew[:, 0:256]], tag="gw")
    ddump("d_gatew", gatew[:, 0:256])

    if KPHASE == "gate":
        S.final_barrier()
        S.emit_all()
        return nc
    for tc in range(4):
        h_phase_tc(0, tc)
    w2_phase(0)
    for e_ in range(1, NE):
        load_expert_w13(e_)
        load_expert_w2(e_)
        for tc in range(4):
            h_phase_tc(e_, tc)
        w2_phase(e_)

    for half in range(2):
        HB = half * 8192
        for dc in range(NDC):
            dma("sp", warena[:, HB + dc * 512: HB + (dc + 1) * 512],
                sw1_bf[half * 512:(half + 1) * 512, dc * 128:(dc + 1) * 128],
                tag=f"sw1T{half}_{dc}", transpose=True)
        for dc in range(NDC):
            dma("sp", warena[:, HB + 4096 + dc * 512: HB + 4096 + (dc + 1) * 512],
                sw3_bf[half * 512:(half + 1) * 512, dc * 128:(dc + 1) * 128],
                tag=f"sw3T{half}_{dc}", transpose=True)
        for tc in range(4):
            for fc in range(4):
                fcg = half * 4 + fc
                k = (tc * 4 + fc) % 2
                p1 = psA if k == 0 else psG
                p3 = psB if k == 0 else psH
                s1x = s1a if k == 0 else s1c
                mm_group(p1[:, :], [(warena[:, HB + dc * 512 + fc * 128: HB + dc * 512 + (fc + 1) * 128],
                                     ln2T(dc, tc * 512, (tc + 1) * 512)) for dc in range(NDC)],
                         f"sh1_{half}_{tc}_{fc}")
                mm_group(p3[:, :], [(warena[:, HB + 4096 + dc * 512 + fc * 128: HB + 4096 + dc * 512 + (fc + 1) * 128],
                                     ln2T(dc, tc * 512, (tc + 1) * 512)) for dc in range(NDC)],
                         f"sh3_{half}_{tc}_{fc}")
                bc1 = bsb1[:, fcg: fcg + 1]
                bc3 = bsb3[:, fcg: fcg + 1]
                act(lambda e, p1=p1, s1x=s1x, bc1=bc1: e.activation(s1x[:, :], p1[:, :], AF.Silu, bias=bc1),
                    rd=[p1[:, :], bc1], wr=[s1x[:, :]], tag=f"ssi_{half}_{tc}_{fc}")
                hd = fmB[:, fcg * 2048 + tc * 512: fcg * 2048 + (tc + 1) * 512]
                dve(lambda e, p3=p3, s1x=s1x, bc3=bc3, hd=hd: e.scalar_tensor_tensor(
                    hd, p3[:, :], bc3, s1x[:, :], ALU.add, ALU.mult),
                    rd=[p3[:, :], bc3, s1x[:, :]], wr=[hd], tag=f"shm_{half}_{tc}_{fc}")
    for fcg in range(8):
        dma("sp", warena[:, fcg * 1024:(fcg + 1) * 1024],
            sw2_bf[0:D, fcg * 128:(fcg + 1) * 128], tag=f"sw2T{fcg}", transpose=True)
    for b4 in range(4):
        tr_group([psF[0:16, i * 128:(i + 1) * 128] for i in range(4)],
                 [gatew[:, (b4 * 4 + i) * 16:(b4 * 4 + i + 1) * 16] for i in range(4)],
                 idf[:, :], f"gtr_{b4}")
        dve(lambda e, b4=b4: e.tensor_copy(gatewT[0:16, b4 * 512:(b4 + 1) * 512], psF[0:16, 0:512]),
            rd=[psF[0:16, 0:512]], wr=[gatewT[0:16, b4 * 512:(b4 + 1) * 512]], tag=f"gtev_{b4}")
    dma("pool", b2bf[:, :], W["L0_ffn_b2"], tag="b2bf")
    dma("pool", brow0, W["L0_ffn_sb2"][None, :], tag="sb2")
    for ts_ in range(NTT):
        lhsts = [fmB[:, fcg * 2048 + ts_ * 128: fcg * 2048 + (ts_ + 1) * 128] for fcg in range(8)]
        rhs_c = [warena[:, fcg * 1024: fcg * 1024 + 512] for fcg in range(8)]
        rhs_d = [warena[:, fcg * 1024 + 512:(fcg + 1) * 1024] for fcg in range(8)]
        gT = gatewT[0:16, ts_ * 128:(ts_ + 1) * 128]
        pc = [(gT, b2bf[0:16, 0:512]), (ones1[0:1, :], brow0[0:1, 0:512])]
        pd = [(gT, b2bf[0:16, 512:1024]), (ones1[0:1, :], brow0[0:1, 512:1024])]
        if KOPT == "nopre":
            pc, pd = [], []
        elif KOPT == "nob2":
            pc, pd = pc[1:], pd[1:]
        elif KOPT == "nosb2":
            pc, pd = pc[:1], pd[:1]
        pc_b = psC if ts_ % 2 == 0 else psF
        pd_b = psD if ts_ % 2 == 0 else psH
        mm_group2(pc_b, pd_b, lhsts, rhs_c, rhs_d, f"sy_{ts_}", pre_c=pc, pre_d=pd)
        xs0 = xres[:, ts_ * D: ts_ * D + 512]
        xs1 = xres[:, ts_ * D + 512: (ts_ + 1) * D]
        dve(lambda e, xs0=xs0, pc_b=pc_b: e.tensor_tensor(xs0, pc_b[:, :], xs0, ALU.add),
            rd=[pc_b[:, :], xs0], wr=[xs0], tag=f"sye0_{ts_}")
        dve(lambda e, xs1=xs1, pd_b=pd_b: e.tensor_tensor(xs1, pd_b[:, :], xs1, ALU.add),
            rd=[pd_b[:, :], xs1], wr=[xs1], tag=f"sye1_{ts_}")

    if KPHASE == "shared":
        S.final_barrier()
        S.emit_all()
        return nc
    for g in range(4):
        dst = out[g * 512:(g + 1) * 512, :].rearrange("(t p) d -> p t d", p=128)
        src = xres[:, g * 4096:(g + 1) * 4096].rearrange("p (t d) -> p t d", d=D)
        dma("act", dst, src, tag=f"xst{g}")

    S.final_barrier()
    cnt = {}
    for o in S.seq:
        cnt[o.eng] = cnt.get(o.eng, 0) + 1
    print("op counts:", cnt, " total", len(S.seq))
    S.emit_all()
    return nc


_NC_CACHE = {}


def kernel(**inputs):
    x = np.ascontiguousarray(np.asarray(inputs["L0_attn_x"], dtype=np.float32))
    if "nc" not in _NC_CACHE:
        _NC_CACHE["nc"] = build_nc()
    nc = _NC_CACHE["nc"]
    warr = {}
    for k, shp in WEIGHT_SHAPES.items():
        a = np.ascontiguousarray(np.asarray(inputs[k], dtype=np.float32)).reshape(shp)
        warr[k] = a
    in_maps = [dict(x_in=x[c], **warr) for c in range(8)]
    res = run_bass_kernel_spmd(nc, in_maps, core_ids=list(range(8)))
    kernel.last_results = res.results
    return np.stack([np.asarray(r["out"]) for r in res.results], axis=0)


if __name__ == "__main__":
    nc = build_nc()
    print("build ok")
